# version 26
# baseline (speedup 1.0000x reference)
import numpy as np
import ml_dtypes
from contextlib import ExitStack

import concourse.bass as bass
import concourse.mybir as mybir
from concourse.bass_utils import run_bass_kernel_spmd

F32 = mybir.dt.float32
BF16 = mybir.dt.bfloat16
I32 = mybir.dt.int32
AF = mybir.ActivationFunctionType
ALU = mybir.AluOpType
AX = mybir.AxisListType

B, L, D = 2, 8192, 2048
H, DKH, DVH = 16, 128, 128
FH = 8192
NCORE = 8
TPC = 2048
HALO = 128
TPAD = TPC + HALO
CH = 128
NCH = TPC // CH
NSL = 16
LNEPS = 1e-5
CLAMP = 1e-6

DEBUG_TAPS = []
DBG_CELL = None
SCAN_FENCE = False
DBG_P4 = None


class Ctr:
    def __init__(self, sem, step):
        self.sem, self.step, self.n = sem, step, 0


class St:
    def __init__(self, key):
        self.key = key
        self.ops = []
        self.waited = {}

    def emit(self, fn, inc=None):
        if inc is None:
            self.ops.append(("op", fn))
            return None
        inc.n += inc.step
        self.ops.append(("inc", fn, inc.sem, inc.step))
        return (inc, inc.n)

    def wait(self, ev):
        if ev is None:
            return
        ctr, v = ev
        if self.waited.get(id(ctr), 0) >= v:
            return
        self.waited[id(ctr)] = v
        self.ops.append(("wait", ctr.sem, v))

    def replay(self, e):
        for it in self.ops:
            if it[0] == "op":
                it[1](e)
            elif it[0] == "inc":
                it[1](e).then_inc(it[2], it[3])
            else:
                e.wait_ge(it[1], it[2])


def emax(*evs):
    out = {}
    for ev in evs:
        if ev is None:
            continue
        c, v = ev
        if out.get(id(c), (None, -1))[1] < v:
            out[id(c)] = (c, v)
    return list(out.values())


class MK:
    def __init__(self):
        self.nc = bass.Bass("TRN2", target_bir_lowering=False)
        self.es = ExitStack()
        self.T = St("T"); self.V = St("V"); self.S = St("S")
        self.P = St("P"); self.Dq = St("D")
        nc = self.nc
        sem = lambda n: self.es.enter_context(nc.semaphore(n))
        self.cT = Ctr(sem("cT"), 1)
        self.cV = Ctr(sem("cV"), 1)
        self.cS = Ctr(sem("cS"), 1)
        self.cP = Ctr(sem("cP"), 1)
        self.dD = Ctr(sem("dD"), 16)
        self.dS = Ctr(sem("dS"), 16)
        self.dP = Ctr(sem("dP"), 16)
        self.cc = Ctr(sem("ccs"), 1)
        self.dbg_out = {}
        self._dmas = {}

    def dq(self, name):
        if name not in self._dmas:
            s = self.es.enter_context(self.nc.semaphore("dma_" + name))
            self._dmas[name] = Ctr(s, 16)
        return self._dmas[name]

    def din(self, name, shape, dt=F32):
        return self.nc.dram_tensor(name, shape, dt, kind="ExternalInput").ap()

    def dout(self, name, shape, dt=F32):
        return self.nc.dram_tensor(name, shape, dt, kind="ExternalOutput").ap()

    def dscr(self, name, shape, dt):
        return self.nc.dram_tensor(name, shape, dt, kind="Internal")

    def sb(self, name, shape, dt):
        return self.es.enter_context(self.nc.sbuf_tensor(name, shape, dt))

    def ps(self, name, shape):
        return self.es.enter_context(self.nc.psum_tensor(name, shape, F32))

    def w(self, stream, *evs):
        for ev in evs:
            if ev is None:
                continue
            if isinstance(ev, list):
                self.w(stream, *ev)
            else:
                stream.wait(ev)


def hadamard(n):
    h = np.array([[1.0]], dtype=np.float64)
    while h.shape[0] < n:
        h = np.block([[h, h], [h, -h]])
    return h


def build_program():
    mk = MK()
    nc, es = mk.nc, mk.es
    T, V, S, P, Dq = mk.T, mk.V, mk.S, mk.P, mk.Dq
    cT, cV, cS, cP, dD, dS, dP = mk.cT, mk.cV, mk.cS, mk.cP, mk.dD, mk.dS, mk.dP

    x_in = mk.din("x", [TPAD, D])
    wq = mk.din("wq", [D, D]); wk = mk.din("wk", [D, D]); wv = mk.din("wv", [D, D])
    wg = mk.din("wg", [D, D]); wo = mk.din("wo", [D, D])
    wb = mk.din("wb", [D, H]); bb = mk.din("bb", [1, H])
    qcw = mk.din("qcw", [D, 4]); qcb = mk.din("qcb", [D])
    kcw = mk.din("kcw", [D, 4]); kcb = mk.din("kcb", [D])
    qnw = mk.din("qnw", [1, DKH]); qnb = mk.din("qnb", [1, DKH])
    knw = mk.din("knw", [1, DKH]); knb = mk.din("knb", [1, DKH])
    w1 = mk.din("w1", [FH, D]); w3 = mk.din("w3", [FH, D]); w2 = mk.din("w2", [D, FH])
    c_ones = mk.din("c_ones", [128, 128])
    c_ul = mk.din("c_ul", [128, 128])
    c_tri = mk.din("c_tri", [128, 128], BF16)
    c_h128 = mk.din("c_h128", [128, 128], BF16)
    c_h64 = mk.din("c_h64", [64, 64], BF16)
    c_pm = mk.din("c_pm", [1, 8])
    c_eye = mk.din("c_eye", [128, 128])

    out = mk.dout("out", [TPC, D])

    xbf = mk.dscr("xbf", [TPAD, D], BF16)
    wqb = mk.dscr("wqb", [D, D], BF16); wkb = mk.dscr("wkb", [D, D], BF16)
    wvb = mk.dscr("wvb", [D, D], BF16); wgb = mk.dscr("wgb", [D, D], BF16)
    wob = mk.dscr("wob", [D, D], BF16); wbb = mk.dscr("wbb", [D, H], BF16)
    w1b = mk.dscr("w1b", [FH, D], BF16); w3b = mk.dscr("w3b", [FH, D], BF16)
    w2b = mk.dscr("w2b", [D, FH], BF16)
    w2t = mk.dscr("w2t", [FH, D], BF16)
    w2f = mk.dscr("w2f", [FH, D], BF16)
    qpre = mk.dscr("qpre", [D, TPC], BF16)
    kpre = mk.dscr("kpre", [D, TPC], BF16)
    qn_s = mk.dscr("qn_s", [TPC, D], BF16)
    kn_s = mk.dscr("kn_s", [TPC, D], BF16)
    v_s = mk.dscr("v_s", [TPC, D], BF16)
    g_s = mk.dscr("g_s", [TPC, D], BF16)
    og_s = mk.dscr("og_s", [TPC, D], BF16)
    x2_s = mk.dscr("x2_s", [TPC, D], BF16)
    qn_f = mk.dscr("qn_f", [D, TPC], BF16)
    kn_f = mk.dscr("kn_f", [D, TPC], BF16)
    og_f = mk.dscr("og_f", [D, TPC], BF16)
    x2_f = mk.dscr("x2_f", [D, TPC], BF16)
    dmat = mk.dscr("dmat", [TPC, D], BF16)
    w1t_f = mk.dscr("w1t_f", [D, FH], BF16)
    w3t_f = mk.dscr("w3t_f", [D, FH], BF16)
    csum_in = mk.dscr("csum_in", [17, DKH * DVH], F32)
    csum_out = mk.dscr("csum_out", [8 * 17, DKH * DVH], F32)

    XF = mk.sb("XF", [128, 16 * TPAD], BF16)
    CT = mk.sb("CT", [128, 20480], BF16)
    WS = mk.sb("WS", [128, 8192], BF16)
    QP = mk.sb("QP", [128, 2 * TPAD + 2 * TPC], BF16)
    AXF = mk.sb("AXF", [128, 4096], F32)
    beta_cm = mk.sb("beta_cm", [128, 256], F32)
    l_cm = mk.sb("l_cm", [128, 256], F32)
    lnb_cm = mk.sb("lnb_cm", [128, 256], F32)
    dec_ncl = mk.sb("dec_ncl", [128, 256], F32)
    dec_g = mk.sb("dec_g", [128, 256], F32)
    dec_phi = mk.sb("dec_phi", [128, 256], F32)
    dec_p = mk.sb("dec_p", [128, 256], F32)
    dec_cum = mk.sb("dec_cum", [128, 256], F32)
    dec_slr = mk.sb("dec_slr", [128, 256], F32)
    phitot = mk.sb("phitot", [128, 16], F32)
    uhat = mk.sb("uhat", [128, 2048], F32)
    SS = mk.sb("SS", [128, 2048], F32)
    SBF = mk.sb("SBF", [128, 2048], BF16)
    ones_f = mk.sb("ones_f", [128, 128], F32)
    ul_f = mk.sb("ul_f", [128, 128], F32)
    tri_b = mk.sb("tri_b", [128, 128], BF16)
    mneg = mk.sb("mneg", [128, 128], F32)
    h128_b = mk.sb("h128_b", [128, 128], BF16)
    h64_b = mk.sb("h64_b", [64, 64], BF16)
    cwq = mk.sb("cwq", [128, 64], F32)
    cwk = mk.sb("cwk", [128, 64], F32)
    cbq = mk.sb("cbq", [128, 16], F32)
    cbk = mk.sb("cbk", [128, 16], F32)
    bbr = mk.sb("bbr", [128, 16], F32)
    gam = mk.sb("gam", [128, 128], BF16)
    bet = mk.sb("bet", [128, 128], BF16)
    smallf = mk.sb("smallf", [128, 192], F32)
    smalli = mk.sb("smalli", [128, 64], I32)
    rowf = mk.sb("rowf", [1, 512], F32)
    eye_f = mk.sb("eye_f", [128, 128], F32)
    pm_r = mk.sb("pm_r", [128, 16], F32)
    phig = mk.sb("phig", [128, 128], F32)
    wcomp = mk.sb("wcomp", [128, 160], F32)
    uldb = mk.sb("uldb", [128, 256], F32)
    dbgf = mk.sb("dbgf", [128, 512], F32)
    dbgb = mk.sb("dbgb", [128, 512], BF16)
    drawb = mk.sb("drawb", [128, 256], F32)
    dmatb = mk.sb("dmatb", [128, 256], BF16)
    qkmb = mk.sb("qkmb", [128, 256], BF16)
    amatb = mk.sb("amatb", [128, 256], BF16)
    ktgb = mk.sb("ktgb", [128, 256], BF16)
    ointb = mk.sb("ointb", [128, 256], F32)

    pb = [mk.ps(f"pb{i}", [128, 512]) for i in range(8)]

    D_ = {}
    D_.update(locals())

    ev = phase0(mk, D_)
    D_["__magic_ev"] = None
    ev = phase1(mk, D_, ev)
    ev = phase3(mk, D_, ev)
    fence_all(mk)
    phase4(mk, D_, ev, ev["w2f_done"])
    emit_debug_taps(mk, D_)

    with mk.es, nc.Block() as block:
        @block.tensor
        def _(e):
            mk.T.replay(e)

        @block.vector
        def _(e):
            mk.V.replay(e)

        @block.scalar
        def _(e):
            mk.S.replay(e)

        @block.gpsimd
        def _(e):
            mk.P.replay(e)

        @block.sync
        def _(e):
            mk.Dq.replay(e)

    n_inst = sum(len(s.ops) for s in (mk.T, mk.V, mk.S, mk.P, mk.Dq))
    print(f"[kernel build] ~{n_inst} stream items")
    return nc


def phase0(mk, D_):
    g = lambda k: D_[k]
    P, Dq, T, V, S = mk.P, mk.Dq, mk.T, mk.V, mk.S
    dP, dD, cT, cV = mk.dP, mk.dD, mk.cT, mk.cV
    ev = {}

    xs, xd = g("x_in"), g("xbf")
    prev = None
    for j in range(4):
        mk.w(P, prev)
        ev[f"xbf{j}"] = P.emit(lambda e, j=j: e.dma_start(
            out=xd[:, j * 512:(j + 1) * 512], in_=xs[:, j * 512:(j + 1) * 512]),
            mk.dq("cast0"))
        prev = ev[f"xbf{j}"]
    ev["xbf"] = prev
    ev["__cast0"] = ev["xbf"]

    cns_names = []
    for dst, src in [("ones_f", "c_ones"), ("ul_f", "c_ul"), ("tri_b", "c_tri"),
                     ("h128_b", "c_h128"), ("h64_b", "c_h64"), ("eye_f", "c_eye")]:
        dt, st_ = g(dst), g(src)
        ev[dst] = S.emit(lambda e, a=dt, b=st_: e.dma_start(out=a[:], in_=b),
                         mk.dq("cns"))
        cns_names.append(dst)
    for dst, src in [("cwq", "qcw"), ("cwk", "kcw")]:
        dt, st_ = g(dst), g(src)
        ev[dst] = S.emit(
            lambda e, a=dt, b=st_: e.dma_start(
                out=a[:, :], in_=b.rearrange("(h dk) j -> dk h j", dk=128)),
            mk.dq("cns"))
        cns_names.append(dst)
    for dst in cns_names:
        ev[dst] = (mk.dq("cns"), mk.dq("cns").n)
    rowf, smallf = g("rowf"), g("smallf")
    prev_use = None
    for dst, src in [("cbq", "qcb"), ("cbk", "kcb")]:
        st_ = g(src)
        mk.w(S, prev_use)
        evl = S.emit(lambda e, b=st_: e.dma_start(
            out=smallf[0:16, 0:128], in_=b.rearrange("(h dk) -> h dk", dk=128)),
            mk.dq("cbl"))
        mk.w(T, evl, ev["eye_f"])
        evm = T.emit(lambda e: e.matmul(D_["pb"][7][:, 0:16], smallf[0:16, 0:128],
                                        D_["eye_f"][0:16, 0:16], start=True, stop=True), cT)
        prev_use = evm
        mk.w(V, evm)
        evc = V.emit(lambda e, a=g(dst): e.tensor_copy(a[:, :], D_["pb"][7][:, 0:16]), cV)
        ev[dst] = evc
        mk.w(T, evc)
    evl = S.emit(lambda e: e.dma_start(out=rowf[0:1, 256:272], in_=D_["bb"]),
                 mk.dq("bbrow"))
    mk.w(T, evl, ev["ones_f"])
    evm = T.emit(lambda e: e.matmul(D_["pb"][7][:, 16:32], D_["ones_f"][0:1, :],
                                    rowf[0:1, 256:272], start=True, stop=True), cT)
    mk.w(V, evm)
    ev["bbr"] = V.emit(lambda e: e.tensor_copy(D_["bbr"][:, :], D_["pb"][7][:, 16:32]), cV)
    mk.w(T, ev["bbr"])
    mk.w(V, ev["tri_b"])
    ev["mneg"] = V.emit(lambda e: e.tensor_scalar(
        D_["mneg"][:, :], D_["tri_b"][:], 100.0, -100.0, ALU.mult, ALU.add), cV)
    return ev


def dmat_pass(mk, D_, ev):
    T, V, S, P = mk.T, mk.V, mk.S, mk.P
    cT, cV, cS = mk.cT, mk.cV, mk.cS
    pb = D_["pb"]
    uldc = [D_["uldb"], D_["drawb"]]
    dtile = [D_["dmatb"][:, 0:128], D_["dmatb"][:, 128:256],
             D_["qkmb"][:, 0:128], D_["qkmb"][:, 128:256]]
    ftile = [D_["dbgf"][:, t * 128:(t + 1) * 128] for t in range(4)]
    dt_st = [None] * 4
    dt_ex = [None] * 4
    mmD2 = [None, None]
    pb_d = [None, None]
    dm_last = []
    for tc in range(16):
        for h2 in range(8):
            p2 = h2 % 2
            eu = []
            mk.w(V, ev["dec_l"], ev["ul_f"], mmD2[p2])
            for hh in range(2):
                h = h2 * 2 + hh
                eu.append(V.emit(lambda e, tc=tc, h=h, hh=hh, p2=p2: e.tensor_scalar(
                    uldc[p2][:, hh * 128:(hh + 1) * 128], D_["ul_f"][:],
                    D_["l_cm"][:, tc * 16 + h:tc * 16 + h + 1], None, ALU.mult), cV))
            mk.w(T, eu, ev["ones_f"], pb_d[p2])
            mm = T.emit(lambda e, p2=p2: e.matmul(
                pb[6 + p2][:, 0:256], D_["ones_f"][:], uldc[p2][:, 0:256],
                start=True, stop=True), cT)
            mmD2[p2] = mm
            newst = []
            for hh in range(2):
                h = h2 * 2 + hh
                t = h % 4
                mk.w(V, mm, ev["mneg"], ev["dec_ncl"], dt_ex[t])
                e_st = V.emit(lambda e, tc=tc, h=h, hh=hh, p2=p2, t=t: e.scalar_tensor_tensor(
                    ftile[t], pb[6 + p2][:, hh * 128:(hh + 1) * 128],
                    D_["dec_ncl"][:, tc * 16 + h:tc * 16 + h + 1], D_["mneg"][:, :],
                    ALU.add, ALU.add), cV)
                newst.append(e_st)
                mk.w(S, e_st, dt_st[t])
                e_ex = S.emit(lambda e, t=t: e.activation(
                    dtile[t], ftile[t], AF.Exp), cS)
                dt_ex[t] = e_ex
                mk.w(P, e_ex)
                e_sv = P.emit(lambda e, tc=tc, h=h, t=t: e.dma_start(
                    out=D_["dmat"].ap()[tc * 128:(tc + 1) * 128, h * 128:(h + 1) * 128],
                    in_=dtile[t]), mk.dq(f"dms{t}"))
                dt_st[t] = e_sv
                if tc == 15 and h >= 12:
                    dm_last.append(e_sv)
            pb_d[p2] = newst
    ev["dmat"] = dm_last
    return ev


def w2f_fold(mk, D_, ev, gate_ev):
    P, Dq, S, T, V = mk.P, mk.Dq, mk.S, mk.T, mk.V
    dP, dD, dS, cT, cV = mk.dP, mk.dD, mk.dS, mk.cT, mk.cV
    CT = D_["QP"]
    t_ev = [gate_ev, gate_ev]
    s_ev = [None, None]
    mm_ev = [None, None]
    it = 0
    for hc in range(64):
        for dg in range(4):
            par = it % 2
            tin = CT[:, par * 512:(par + 1) * 512]
            tout = CT[:, 1024 + par * 512:1024 + (par + 1) * 512]
            mk.w(Dq, ev["w2b"], mm_ev[par], gate_ev)
            ld = Dq.emit(lambda e, tin=tin, hc=hc, dg=dg: e.dma_start(
                out=tin, in_=D_["w2b"].ap()[dg * 512:(dg + 1) * 512,
                                            hc * 128:(hc + 1) * 128],
                transpose=True), mk.dq(f"w2f_l{par}"))
            mk.w(T, ld, ev["h128_b"], t_ev[par])
            evm = T.emit(lambda e, tin=tin, par=par: e.matmul(
                D_["pb"][par][:, :], D_["h128_b"][:], tin, start=True, stop=True), cT)
            mm_ev[par] = evm
            mk.w(V, evm, s_ev[par])
            evc = V.emit(lambda e, tout=tout, par=par: e.tensor_copy(
                tout, D_["pb"][par][:, :]), cV)
            t_ev[par] = evc
            mk.w(P, evc)
            s_ev[par] = P.emit(lambda e, tout=tout, hc=hc, dg=dg: e.dma_start(
                out=D_["w2t"].ap()[hc * 128:(hc + 1) * 128, dg * 512:(dg + 1) * 512],
                in_=tout), mk.dq(f"w2f_s{par}"))
            it += 1
    last_s1 = emax(s_ev[0]) + emax(s_ev[1])
    w2t_v = D_["w2t"].ap().rearrange("(h1 h2) d -> h1 h2 d", h2=128)
    w2f_v = D_["w2f"].ap().rearrange("(h1 h2) d -> h1 h2 d", h2=128)
    t_ev = [None, None]; s_ev = [None, None]; mm_ev = [None, None]
    it = 0
    for h2b in range(0, 128, 4):
        for dg in range(4):
            par = it % 2
            tin = CT[0:64, par * 2048:(par + 1) * 2048]
            tout = CT[0:64, 4096 + par * 2048:4096 + (par + 1) * 2048]
            mk.w(S, last_s1, mm_ev[par], gate_ev)
            ld = S.emit(lambda e, tin=tin, h2b=h2b, dg=dg: e.dma_start(
                out=tin.rearrange("p (h2 d) -> p h2 d", d=512),
                in_=w2t_v[:, h2b:h2b + 4, dg * 512:(dg + 1) * 512]), mk.dq(f"w2f2_l{par}"))
            for j in range(4):
                mk.w(T, ld, ev["h64_b"], t_ev[par])
                evm = T.emit(lambda e, tin=tin, j=j: e.matmul(
                    D_["pb"][2 + j][0:64, :], D_["h64_b"][:],
                    tin[:, j * 512:(j + 1) * 512], start=True, stop=True), cT)
                if j == 3:
                    mm_ev[par] = evm
                mk.w(V, evm, s_ev[par])
                evc = V.emit(lambda e, tout=tout, j=j: e.tensor_copy(
                    tout[:, j * 512:(j + 1) * 512], D_["pb"][2 + j][0:64, :]), cV)
                t_ev[par] = evc
            mk.w(P, t_ev[par])
            s_ev[par] = P.emit(lambda e, tout=tout, h2b=h2b, dg=dg: e.dma_start(
                out=w2f_v[:, h2b:h2b + 4, dg * 512:(dg + 1) * 512],
                in_=tout.rearrange("p (h2 d) -> p h2 d", d=512)), mk.dq(f"w2f2_s{par}"))
            it += 1
    return emax(s_ev[0], s_ev[1])


GRPS = [(0, 512), (512, 512), (1024, 512), (1536, 512), (2048, 128)]


def fence(mk):
    cs = [mk.cT, mk.cV, mk.cS, mk.cP, mk.dD, mk.dS, mk.dP] + [
        c for n, c in mk._dmas.items() if not n.startswith("cast")]
    return [(c, c.n) for c in cs if c.n > 0]


def fence_all(mk):
    F = fence(mk)
    for st_ in (mk.T, mk.V, mk.S, mk.P, mk.Dq):
        mk.w(st_, F)
    return F


def phase1(mk, D_, ev):
    g = lambda k: D_[k]
    T, V, S, P, Dq = mk.T, mk.V, mk.S, mk.P, mk.Dq
    cT, cV, cS, cP, dD, dS, dP = mk.cT, mk.cV, mk.cS, mk.cP, mk.dD, mk.dS, mk.dP
    XF, CT, WS, QP, AXF = g("XF"), g("CT"), g("WS"), g("QP"), g("AXF")
    smallf, smalli = g("smallf"), g("smalli")
    pb = g("pb")

    for dc in range(16):
        mk.w(Dq, ev[f"xbf{dc // 4}"])
        e1 = Dq.emit(lambda e, dc=dc: e.dma_start(
            out=XF[:, dc * TPAD:(dc + 1) * TPAD],
            in_=D_["xbf"].ap()[:, dc * 128:(dc + 1) * 128], transpose=True),
            mk.dq("xfm"))
    ev["xfm"] = e1

    ev["magic"] = V.emit(lambda e: e.memset(smalli[:, 16:32], 0x5F3759DF), cV)
    D_["__magic_ev"] = ev["magic"]

    RAWW = {"wvb": "wv", "wgb": "wg", "wkb": "wk", "wqb": "wq"}

    def tm_pass(wname, dst, silu, with_beta):
        wsrc = D_[RAWW[wname]]
        wbuf = [WS[:, 0:8192], CT[:, 0:8192]]
        ebuf = [CT[:, 8448 + i * 512:8448 + (i + 1) * 512] for i in range(2)]
        wld = [None, None]; wuse = [None, None]
        bank_free = {0: None, 1: None, 2: None}
        ebuf_free = [None, None]
        bet_ev = None
        last_store = None
        evbw = None
        if with_beta:
            evbw = P.emit(lambda e: e.dma_start(
                out=CT[:, 8192:8448].rearrange("p (dc n) -> p dc n", n=16),
                in_=D_["wb"].rearrange("(dc p) n -> p dc n", p=128)), mk.dq("wbb1"))
        it = 0
        for ob in range(4):
            par = ob % 2
            mk.w(P, wuse[par])
            wld[par] = P.emit(lambda e, par=par, ob=ob: e.dma_start(
                out=wbuf[par].rearrange("p (dc n) -> p dc n", n=512),
                in_=wsrc.rearrange("(dc p) n -> p dc n", p=128)[:, :, ob * 512:(ob + 1) * 512]),
                mk.dq(f"tmw{par}"))
            for tc in range(16):
                bank = it % 2
                mk.w(T, ev["xfm"], wld[par], bank_free[bank])
                for dc in range(16):
                    evm = T.emit(lambda e, dc=dc, tc=tc, par=par, bank=bank: e.matmul(
                        pb[bank][:, :],
                        XF[:, dc * TPAD + HALO + tc * 128:dc * TPAD + HALO + (tc + 1) * 128],
                        wbuf[par][:, dc * 512:(dc + 1) * 512],
                        start=(dc == 0), stop=(dc == 15)), cT if dc == 15 else None)
                wuse[par] = evm
                ep = it % 2
                if silu:
                    mk.w(S, evm, ebuf_free[ep])
                    evc = S.emit(lambda e, bank=bank, ep=ep: e.activation(
                        ebuf[ep], pb[bank][:, :], AF.Silu), cS)
                else:
                    mk.w(V, evm, ebuf_free[ep])
                    evc = V.emit(lambda e, bank=bank, ep=ep: e.tensor_copy(
                        ebuf[ep], pb[bank][:, :]), cV)
                bank_free[bank] = evc
                mk.w(P, evc)
                est = P.emit(lambda e, ep=ep, tc=tc, ob=ob: e.dma_start(
                    out=D_[dst].ap()[tc * 128:(tc + 1) * 128, ob * 512:(ob + 1) * 512],
                    in_=ebuf[ep]), mk.dq(f"tmst{ep}"))
                ebuf_free[ep] = est
                last_store = est
                it += 1
        if with_beta:
            for tc in range(16):
                mk.w(T, evbw, bank_free[2])
                for dc in range(16):
                    evm = T.emit(lambda e, dc=dc, tc=tc: e.matmul(
                        pb[2][:, 0:16],
                        XF[:, dc * TPAD + HALO + tc * 128:dc * TPAD + HALO + (tc + 1) * 128],
                        CT[:, 8192 + dc * 16:8192 + (dc + 1) * 16],
                        start=(dc == 0), stop=(dc == 15)), cT if dc == 15 else None)
                mk.w(V, evm)
                evc = V.emit(lambda e, tc=tc: e.tensor_copy(
                    D_["beta_cm"][:, tc * 16:(tc + 1) * 16], pb[2][:, 0:16]), cV)
                bank_free[2] = evc
                bet_ev = evc
        return last_store, bet_ev

    ev["v_s"], ev["betaz"] = tm_pass("wvb", "v_s", silu=False, with_beta=True)
    fence_all(mk)
    ev["w2b"] = P.emit(lambda e: e.dma_start(out=D_["w2b"].ap()[:, :], in_=g("w2")),
                       mk.dq("cast1"))

    bcm, lcm, bclamp = D_["beta_cm"], D_["l_cm"], D_["lnb_cm"]
    dec_ncl, dec_g = D_["dec_ncl"], D_["dec_g"]
    dec_phi, dec_p, dec_cum, dec_slr = D_["dec_phi"], D_["dec_p"], D_["dec_cum"], D_["dec_slr"]
    e_pv = None
    for tc in range(16):
        mk.w(V, ev["betaz"], ev["bbr"], e_pv)
        e_pv = V.emit(lambda e, tc=tc: e.tensor_tensor(
            bcm[:, tc * 16:(tc + 1) * 16], bcm[:, tc * 16:(tc + 1) * 16],
            D_["bbr"][:, :], ALU.add), cV)
    mk.w(S, e_pv)
    e_sig = S.emit(lambda e: e.activation(bcm[:, :], bcm[:, :], AF.Sigmoid), cS)
    mk.w(V, e_sig)
    e_a1 = V.emit(lambda e: e.tensor_scalar(lcm[:, :], bcm[:, :], -1.0, 1.0,
                                            ALU.mult, ALU.add), cV)
    mk.w(V, e_a1)
    e_a2 = V.emit(lambda e: e.tensor_scalar_max(lcm[:, :], lcm[:, :], CLAMP), cV)
    e_b1 = V.emit(lambda e: e.tensor_scalar_max(bclamp[:, :], bcm[:, :], CLAMP), cV)
    mk.w(S, e_a2, e_b1)
    e_l = S.emit(lambda e: e.activation(lcm[:, :], lcm[:, :], AF.Ln), cS)
    e_lnb = S.emit(lambda e: e.activation(bclamp[:, :], bclamp[:, :], AF.Ln), cS)
    ecv = None
    for tc in range(16):
        mk.w(T, e_l, ev["ul_f"], ev["ones_f"], ecv)
        T.emit(lambda e, tc=tc: e.matmul(
            pb[3][:, 0:16], D_["ul_f"][:], lcm[:, tc * 16:(tc + 1) * 16],
            start=True, stop=True), None)
        em2 = T.emit(lambda e, tc=tc: e.matmul(
            pb[3][:, 16:32], D_["ones_f"][:], lcm[:, tc * 16:(tc + 1) * 16],
            start=True, stop=True), cT)
        mk.w(V, em2)
        V.emit(lambda e, tc=tc: e.tensor_copy(
            dec_cum[:, tc * 16:(tc + 1) * 16], pb[3][:, 0:16]), None)
        ecv = V.emit(lambda e, tc=tc: e.tensor_copy(
            dec_slr[:, tc * 16:(tc + 1) * 16], pb[3][:, 16:32]), cV)
    mk.w(V, e_lnb, ecv)
    e_ncl = V.emit(lambda e: e.tensor_sub(dec_ncl[:, :], bclamp[:, :], dec_cum[:, :]), cV)
    mk.w(V, e_ncl)
    e_garg = V.emit(lambda e: e.tensor_add(dec_g[:, :], dec_slr[:, :], dec_ncl[:, :]), cV)
    mk.w(S, e_garg, ecv)
    e_g = S.emit(lambda e: e.activation(dec_g[:, :], dec_g[:, :], AF.Exp), cS)
    e_phi = S.emit(lambda e: e.activation(dec_phi[:, :], dec_slr[:, :], AF.Exp), cS)
    e_p = S.emit(lambda e: e.activation(dec_p[:, :], dec_cum[:, :], AF.Exp), cS)
    mk.w(V, ecv)
    e_pt = V.emit(lambda e: e.reduce_sum(
        D_["phitot"][:, :],
        dec_slr[:, :].rearrange("p (c h) -> p h c", h=16), axis=AX.X), cV)
    mk.w(S, e_pt)
    e_pt2 = S.emit(lambda e: e.activation(D_["phitot"][:, :], D_["phitot"][:, :], AF.Exp), cS)
    ev["dec_g"] = e_g; ev["dec_phi"] = e_phi; ev["dec_p"] = e_p
    ev["dec_ncl"] = e_ncl; ev["dec_l"] = e_l; ev["phitot"] = e_pt2
    fence_all(mk)

    def fm_pass(wname, cw, cb, gamsrc, betsrc, rowbase, predst, lndst, do_u):
        wsrc = D_[RAWW[wname]]
        wbuf = [WS[:, 0:2048], WS[:, 2048:4096]]
        qsl = [QP[:, 0:TPAD], QP[:, TPAD:2 * TPAD]]
        acc = [QP[:, 2 * TPAD:2 * TPAD + 2048], QP[:, 2 * TPAD + 2048:2 * TPAD + 4096]]
        wld = [None, None]; wuse = [None, None]
        bank_free = {0: None, 1: None}
        conv_done = [None, None]
        store_ev = [None, None]
        gam, bet = D_["gam"], D_["bet"]
        evl = S.emit(lambda e: e.dma_start(
            out=D_["rowf"][0:1, 0:128], in_=D_[gamsrc]), mk.dq("rowg1"))
        evl2 = S.emit(lambda e: e.dma_start(
            out=D_["rowf"][0:1, 128:256], in_=D_[betsrc]), mk.dq("rowg2"))
        mk.w(T, evl, evl2, ev["ones_f"])
        eg1 = T.emit(lambda e: e.matmul(
            pb[5][:, 0:128], D_["ones_f"][0:1, :],
            D_["rowf"][0:1, 0:128], start=True, stop=True), cT)
        eg2 = T.emit(lambda e: e.matmul(
            pb[5][:, 128:256], D_["ones_f"][0:1, :],
            D_["rowf"][0:1, 128:256], start=True, stop=True), cT)
        mk.w(V, eg2)
        ec1 = V.emit(lambda e: e.tensor_copy(gam[:, :], pb[5][:, 0:128]), cV)
        e_gb = V.emit(lambda e: e.tensor_copy(bet[:, :], pb[5][:, 128:256]), cV)

        CE, cC = V, cV

        for s in range(16):
            par = s % 2
            mk.w(P, wuse[par])
            wld[par] = P.emit(lambda e, par=par, s=s: e.dma_start(
                out=wbuf[par].rearrange("p (dc n) -> p dc n", n=128),
                in_=wsrc.rearrange("(dc p) n -> p dc n", p=128)[:, :, s * 128:(s + 1) * 128]),
                mk.dq(f"fmw{par}"))
            for gi, (g0, gw) in enumerate(GRPS):
                bank = (s * len(GRPS) + gi) % 2
                mk.w(T, ev["xfm"], wld[par], bank_free[bank])
                for dc in range(16):
                    evm = T.emit(lambda e, dc=dc, g0=g0, gw=gw, par=par, bank=bank: e.matmul(
                        pb[bank][:, 0:gw],
                        wbuf[par][:, dc * 128:(dc + 1) * 128],
                        XF[:, dc * TPAD + g0:dc * TPAD + g0 + gw],
                        start=(dc == 0), stop=(dc == 15)), cT if dc == 15 else None)
                wuse[par] = evm
                mk.w(V, evm, conv_done[par], store_ev[par])
                bank_free[bank] = V.emit(lambda e, bank=bank, par=par, g0=g0, gw=gw: e.tensor_copy(
                    qsl[par][:, g0:g0 + gw], pb[bank][:, 0:gw]), cV)
            mk.w(CE, bank_free[0], bank_free[1], ev[cw], store_ev[par])
            prev = CE.emit(lambda e, par=par, s=s: e.tensor_scalar(
                acc[par][:, :], qsl[par][:, HALO - 3:HALO - 3 + 2048],
                D_[cw][:, s * 4:s * 4 + 1], None, ALU.mult), cC)
            for j in range(1, 4):
                mk.w(CE, prev)
                prev = CE.emit(lambda e, par=par, s=s, j=j: e.scalar_tensor_tensor(
                    acc[par][:, :], qsl[par][:, HALO - 3 + j:HALO - 3 + j + 2048],
                    D_[cw][:, s * 4 + j:s * 4 + j + 1], acc[par][:, :],
                    ALU.mult, ALU.add), cC)
            conv_done[par] = prev
            mk.w(S, prev, ev[cb])
            e_silu = S.emit(lambda e, par=par, s=s: e.activation(
                qsl[par][:, 0:2048], acc[par][:, :], AF.Silu,
                bias=D_[cb][:, s:s + 1]), cS)
            STq, dSTq = S, dS
            mk.w(STq, e_silu)
            store_ev[par] = STq.emit(lambda e, par=par, s=s: e.dma_start(
                out=D_[predst].ap()[s * 128:(s + 1) * 128, :],
                in_=qsl[par][:, 0:2048]), mk.dq(f"fmst{par}"))
        all_stores = store_ev[1]

        tmin = [CT[:, 0 + i * 2048:0 + (i + 1) * 2048] for i in range(2)]
        tmo1 = [CT[:, 4096 + i * 2048:4096 + (i + 1) * 2048] for i in range(2)]
        tmo2 = [CT[:, 8192 + i * 2048:8192 + (i + 1) * 2048] for i in range(2)]
        vch = [CT[:, 12288 + i * 2048:12288 + (i + 1) * 2048] for i in range(2)]
        ktg = D_["ktgb"]
        tmin_use = [None, None]
        tmo1_use = [None, None]
        tmo2_use = [None, None]
        vld = [None, None]; vuse = [None, None]
        e_sq_use = None
        uev = {h: None for h in range(16)}
        prev_kt = None
        prev_um = None
        prev_stt = None
        if do_u:
            e_u0 = V.emit(lambda e: e.memset(D_["uhat"][:, :], 0.0), cV)
            for h in range(16):
                uev[h] = e_u0
        last_st = None
        for tc in range(16):
            par = tc % 2
            mk.w(Dq, all_stores, store_ev[0], tmin_use[par])
            ld = Dq.emit(lambda e, tc=tc, par=par: e.dma_start(
                out=tmin[par], in_=D_[predst].ap()[:, tc * 128:(tc + 1) * 128],
                transpose=True), mk.dq(f"tmin{par}"))
            if do_u:
                mk.w(S, ev["v_s"], vuse[par])
                vld[par] = S.emit(lambda e, tc=tc, par=par: e.dma_start(
                    out=vch[par], in_=D_["v_s"].ap()[tc * 128:(tc + 1) * 128, :]),
                    mk.dq(f"vch{par}"))
            mk.w(S, ld, e_sq_use)
            e_sq = S.emit(lambda e, par=par: e.activation(
                AXF[:, 0:2048], tmin[par], AF.Square), cS)
            mk.w(V, ld)
            e_r1 = V.emit(lambda e, par=par: e.reduce_sum(
                smallf[:, 0:16], tmin[par].rearrange("p (h d) -> p h d", d=128),
                axis=AX.X), cV)
            mk.w(V, e_sq)
            e_r2 = V.emit(lambda e: e.reduce_sum(
                smallf[:, 16:32], AXF[:, 0:2048].rearrange("p (h d) -> p h d", d=128),
                axis=AX.X), cV)
            e_sq_use = e_r2
            mk.w(V, e_r1)
            e_m = V.emit(lambda e: e.tensor_scalar_mul(
                smallf[:, 32:48], smallf[:, 0:16], 1.0 / 128), cV)
            mk.w(V, e_r2)
            e_x2 = V.emit(lambda e: e.tensor_scalar_mul(
                smallf[:, 48:64], smallf[:, 16:32], 1.0 / 128), cV)
            mk.w(V, e_m)
            e_m2 = V.emit(lambda e: e.tensor_mul(
                smallf[:, 64:80], smallf[:, 32:48], smallf[:, 32:48]), cV)
            mk.w(V, e_m2, e_x2)
            e_var = V.emit(lambda e: e.tensor_sub(
                smallf[:, 80:96], smallf[:, 48:64], smallf[:, 64:80]), cV)
            mk.w(V, e_var)
            e_vr = V.emit(lambda e: e.tensor_scalar_add(
                smallf[:, 80:96], smallf[:, 80:96], LNEPS), cV)
            mk.w(V, e_vr, ev["magic"])
            e_i1 = V.emit(lambda e: e.tensor_scalar(
                smalli[:, 0:16], smallf[:, 80:96].bitcast(I32), 1, None,
                ALU.arith_shift_right), cV)
            mk.w(V, e_i1)
            e_prev = V.emit(lambda e: e.tensor_sub(
                smalli[:, 32:48], smalli[:, 16:32], smalli[:, 0:16]), cV)
            ycur = smalli[:, 32:48].bitcast(F32)
            for itn in range(2):
                mk.w(V, e_prev)
                e_y2 = V.emit(lambda e, ycur=ycur: e.tensor_mul(
                    smallf[:, 96:112], ycur, ycur), cV)
                mk.w(V, e_y2)
                e_t = V.emit(lambda e: e.tensor_mul(
                    smallf[:, 96:112], smallf[:, 96:112], smallf[:, 80:96]), cV)
                mk.w(V, e_t)
                e_t2 = V.emit(lambda e: e.tensor_scalar(
                    smallf[:, 96:112], smallf[:, 96:112], -0.5, 1.5,
                    ALU.mult, ALU.add), cV)
                mk.w(V, e_t2)
                e_prev = V.emit(lambda e, ycur=ycur: e.tensor_mul(
                    smallf[:, 112:128], ycur, smallf[:, 96:112]), cV)
                ycur = smallf[:, 112:128]
            e_rstd = e_prev
            mk.w(V, e_rstd, tmo1_use[par])
            ap_ev = None
            for h in range(16):
                ap_ev = V.emit(lambda e, par=par, h=h: e.tensor_scalar(
                    tmo1[par][:, h * 128:(h + 1) * 128], tmin[par][:, h * 128:(h + 1) * 128],
                    smallf[:, 32 + h:33 + h], smallf[:, 112 + h:113 + h],
                    ALU.subtract, ALU.mult), cV)
            tmin_use[par] = [ap_ev, e_sq]
            mk.w(P, ap_ev, e_gb, tmo2_use[par])
            e_gm = None
            for h in range(16):
                e_gm = P.emit(lambda e, par=par, h=h: e.tensor_mul(
                    tmo2[par][:, h * 128:(h + 1) * 128],
                    tmo1[par][:, h * 128:(h + 1) * 128], gam[:, :]), cP)
            mk.w(P, e_gm)
            e_bt = None
            for h in range(16):
                e_bt = P.emit(lambda e, par=par, h=h: e.tensor_add(
                    tmo2[par][:, h * 128:(h + 1) * 128],
                    tmo2[par][:, h * 128:(h + 1) * 128], bet[:, :]), cP)
            tmo1_use[par] = e_bt
            mk.w(S, e_bt)
            est = S.emit(lambda e, tc=tc, par=par: e.dma_start(
                out=D_[lndst].ap()[tc * 128:(tc + 1) * 128, :], in_=tmo2[par]),
                mk.dq(f"lns{par}"))
            tmo2_use[par] = [est]
            last_st = est
            if do_u:
                for h in range(16):
                    mk.w(V, e_bt, ev["dec_g"], prev_um)
                    e_kt = V.emit(lambda e, par=par, h=h, tc=tc: e.tensor_scalar(
                        ktg[:, 0:128], tmo2[par][:, h * 128:(h + 1) * 128],
                        D_["dec_g"][:, tc * 16 + h:tc * 16 + h + 1], None, ALU.mult), cV)
                    prev_kt = e_kt
                    mk.w(T, e_kt, vld[par], prev_stt)
                    e_um = T.emit(lambda e, par=par, h=h: e.matmul(
                        pb[4][:, 0:128], ktg[:, 0:128], vch[par][:, h * 128:(h + 1) * 128],
                        start=True, stop=True), cT)
                    prev_um = e_um
                    vuse[par] = e_um
                    mk.w(V, e_um, ev["dec_phi"], uev[h])
                    uev[h] = prev_stt = V.emit(lambda e, h=h, tc=tc: e.scalar_tensor_tensor(
                        D_["uhat"][:, h * 128:(h + 1) * 128],
                        D_["uhat"][:, h * 128:(h + 1) * 128],
                        D_["dec_phi"][:, tc * 16 + h:tc * 16 + h + 1],
                        pb[4][:, 0:128], ALU.mult, ALU.add), cV)
                tmo2_use[par].append(prev_kt)
        return last_st, (uev if do_u else None)

    ev["kn_s"], uev = fm_pass("wkb", "cwk", "cbk", "knw", "knb", 2, "kpre", "kn_s", True)
    ev["uhat"] = emax(*uev.values())
    fence_all(mk)
    phase2a(mk, D_, ev)
    ev = dmat_pass(mk, D_, ev)
    mbk = [D_["CT"][:, 16384 + i * 2048:16384 + (i + 1) * 2048] for i in range(2)]
    ev["kn_f"] = mirror_pass(mk, D_, "kn_s", "kn_f", 16, mbk, ev["kn_s"])
    ev["w2f_done"] = w2f_fold(mk, D_, ev, None)
    fence_all(mk)
    ev["qn_s"], _ = fm_pass("wqb", "cwq", "cbq", "qnw", "qnb", 0, "qpre", "qn_s", False)
    fence_all(mk)
    ev["w1b"] = P.emit(lambda e: e.dma_start(out=D_["w1b"].ap()[:, :], in_=g("w1")),
                       mk.dq("cast2"))
    ev["w3b"] = P.emit(lambda e: e.dma_start(out=D_["w3b"].ap()[:, :], in_=g("w3")),
                       mk.dq("cast3"))
    ev["g_s"], _ = tm_pass("wgb", "g_s", silu=True, with_beta=False)
    ev = phase2(mk, D_, ev)
    ev["gate_end"] = (mk.cT, mk.cT.n)
    ev["gate_end_v"] = (mk.cV, mk.cV.n)
    ev["gate_end_s"] = (mk.cS, mk.cS.n)
    return ev


def phase2a(mk, D_, ev):
    S, P = mk.S, mk.P
    mk.w(S, ev["phitot"])
    e_s1 = S.emit(lambda e: e.dma_start(
        out=D_["csum_in"].ap()[0:1, 0:16], in_=D_["phitot"][0:1, 0:16]), mk.dq("cs1"))
    mk.w(S, ev["uhat"])
    e_s2 = S.emit(lambda e: e.dma_start(
        out=D_["csum_in"].ap()[1:17, :].rearrange("h (dk dv) -> dk h dv", dk=128, dv=128),
        in_=D_["uhat"][:, :].rearrange("p (h dv) -> p h dv", dv=128)), mk.dq("cs2"))
    mk.w(P, e_s1, e_s2)
    ev["cc"] = P.emit(lambda e: e.collective_compute(
        "AllGather", ALU.bypass, replica_groups=[list(range(8))],
        ins=[D_["csum_in"].ap().opt()], outs=[D_["csum_out"].ap().opt()]), mk.cc)
    return ev


def phase2(mk, D_, ev):
    g = lambda k: D_[k]
    T, V, S, P, Dq = mk.T, mk.V, mk.S, mk.P, mk.Dq
    cT, cV, cS, cP, dD, dP = mk.cT, mk.cV, mk.cS, mk.cP, mk.dD, mk.dP
    dS = mk.dS
    pb, AXF = g("pb"), g("AXF")
    e_cc = ev["cc"]

    mk.w(S, e_cc)
    e_l1 = S.emit(lambda e: e.dma_start(
        out=D_["rowf"][0:1, 280:408].rearrange("s (r n) -> s r n", n=16),
        in_=D_["csum_out"].ap().rearrange("(r s) n -> s r n", s=17)[0:1, :, 0:16]), mk.dq("cl1"))
    e_l2 = S.emit(lambda e: e.dma_start(out=D_["rowf"][0:1, 272:280], in_=D_["c_pm"]), mk.dq("cl2"))
    mk.w(T, e_l1, e_l2, ev["ones_f"])
    e_m1 = T.emit(lambda e: e.matmul(pb[5][:, 0:128], D_["ones_f"][0:1, :],
                                     D_["rowf"][0:1, 280:408], start=True, stop=True), cT)
    e_m2 = T.emit(lambda e: e.matmul(pb[5][:, 128:136], D_["ones_f"][0:1, :],
                                     D_["rowf"][0:1, 272:280], start=True, stop=True), cT)
    mk.w(V, e_m1, e_m2)
    e_pg = V.emit(lambda e: e.tensor_copy(D_["phig"][:, :], pb[5][:, 0:128]), cV)
    e_pm = V.emit(lambda e: e.tensor_copy(D_["pm_r"][:, 0:8], pb[5][:, 128:136]), cV)
    mk.w(V, e_pm)
    e_pm2 = V.emit(lambda e: e.tensor_scalar(D_["pm_r"][:, 8:16], D_["pm_r"][:, 0:8],
                                             -1.0, 1.0, ALU.mult, ALU.add), cV)
    wc = D_["wcomp"]
    e_f = V.emit(lambda e: e.memset(wc[:, 128:144], 1.0), cV)
    cur, nxt = 144, 128
    cur = 128
    nxt = 144
    for c in range(7, -1, -1):
        mk.w(V, e_f, e_pg, e_pm2)
        e_w = V.emit(lambda e, c=c, cur=cur: e.tensor_scalar(
            wc[:, c * 16:(c + 1) * 16], wc[:, cur:cur + 16],
            D_["pm_r"][:, c:c + 1], None, ALU.mult), cV)
        mk.w(V, e_w)
        e_t = V.emit(lambda e, c=c: e.tensor_scalar(
            D_["smallf"][:, 128:144], D_["phig"][:, c * 16:(c + 1) * 16],
            D_["pm_r"][:, c:c + 1], D_["pm_r"][:, 8 + c:9 + c],
            ALU.mult, ALU.add), cV)
        mk.w(V, e_t)
        e_f = V.emit(lambda e, cur=cur, nxt=nxt: e.tensor_mul(
            wc[:, nxt:nxt + 16], wc[:, cur:cur + 16], D_["smallf"][:, 128:144]), cV)
        cur, nxt = nxt, cur
    e_wdone = e_f

    e_sa0 = V.emit(lambda e: e.memset(D_["SS"][:, :], 0.0), cV)
    csv = D_["csum_out"].ap().rearrange("(r s) n -> r s n", s=17)
    prev_ld = None
    prev_stt = e_sa0
    for c in range(8):
        mk.w(S, e_cc, prev_stt)
        e_ld = S.emit(lambda e, c=c: e.dma_start(
            out=AXF[:, 0:2048].rearrange("p (h dv) -> p h dv", dv=128),
            in_=csv[c, 1:17, :].rearrange("h (dk dv) -> dk h dv", dk=128, dv=128)), mk.dq("cug"))
        for h in range(16):
            mk.w(V, e_ld, e_wdone, prev_stt)
            prev_stt = V.emit(lambda e, c=c, h=h: e.scalar_tensor_tensor(
                D_["SS"][:, h * 128:(h + 1) * 128], AXF[:, h * 128:(h + 1) * 128],
                wc[:, c * 16 + h:c * 16 + h + 1], D_["SS"][:, h * 128:(h + 1) * 128],
                ALU.mult, ALU.add), cV)
        prev_ld = e_ld
    mk.w(V, prev_stt)
    e_sbf = V.emit(lambda e: e.tensor_copy(D_["SBF"][:, :], D_["SS"][:, :]), cV)
    ev["SBF0"] = e_sbf
    return ev


def mirror_pass(mk, D_, src_name, dst_name, ncols, buf, gate, nrows=TPC, stq=None,
                dual=False):
    Dq, S = mk.Dq, (stq if stq is not None else mk.S)
    bufs = buf if isinstance(buf, list) else [buf]
    use = [None] * len(bufs)
    evs = []
    for qc in range(ncols):
        p = qc % len(bufs)
        trq = (mk.Dq if qc % 2 == 0 else mk.S) if dual else Dq
        sq = (mk.S if qc % 2 == 0 else mk.Dq) if dual else S
        mk.w(trq, gate, use[p])
        ld = trq.emit(lambda e, qc=qc, p=p: e.dma_start(
            out=bufs[p][:, 0:nrows],
            in_=D_[src_name].ap()[:, qc * 128:(qc + 1) * 128], transpose=True),
            mk.dq(f"mirl{p}"))
        mk.w(sq, ld)
        st = sq.emit(lambda e, qc=qc, p=p: e.dma_start(
            out=D_[dst_name].ap()[qc * 128:(qc + 1) * 128, :], in_=bufs[p][:, 0:nrows]),
            mk.dq(f"mirs{p}"))
        use[p] = st
        evs = use
    return [e for e in evs if e is not None]


def phase3(mk, D_, ev):
    g = lambda k: D_[k]
    T, V, S, P, Dq = mk.T, mk.V, mk.S, mk.P, mk.Dq
    cT, cV, cS, cP, dD, dS, dP = mk.cT, mk.cV, mk.cS, mk.cP, mk.dD, mk.dS, mk.dP
    XF, CT, AXF, pb = g("XF"), g("CT"), g("AXF"), g("pb")
    uld, drb, dmb, qkb, amb, ktg, oib = (g("uldb"), g("drawb"), g("dmatb"),
                                         g("qkmb"), g("amatb"), g("ktgb"), g("ointb"))
    mk.w(T, ev["gate_end_v"], ev["gate_end_s"])
    mb = [CT[:, 16384 + i * 2048:16384 + (i + 1) * 2048] for i in range(2)]
    ev["qn_f"] = mirror_pass(mk, D_, "qn_s", "qn_f", 16, mb, [ev["qn_s"], ev["kn_f"]])
    wb1 = XF[:, 24576:32768]
    ev["w1t_f"] = mirror_pass(mk, D_, "w1b", "w1t_f", 16, wb1,
                              [ev["w1b"], ev["qn_f"], ev["gate_end"]], nrows=FH,
                              stq=mk.P)
    ev["w3t_f"] = mirror_pass(mk, D_, "w3b", "w3t_f", 16, wb1,
                              [ev["w3b"], ev["w1t_f"]], nrows=FH, stq=mk.P)

    qf = [CT[:, 0:4096], CT[:, 4096:8192]]
    kf = [CT[:, 8192:12288], CT[:, 12288:16384]]
    gch = [XF[:, i * 2048:(i + 1) * 2048] for i in range(2)]
    ogb = [XF[:, 4096 + i * 2048:4096 + (i + 1) * 2048] for i in range(2)]
    obf = [XF[:, 8192 + i * 2048:8192 + (i + 1) * 2048] for i in range(2)]
    vch = [XF[:, 12288 + i * 2048:12288 + (i + 1) * 2048] for i in range(2)]
    ktm = [XF[:, 16384 + i * 2048:16384 + (i + 1) * 2048] for i in range(2)]
    amb4 = [XF[:, 20480 + i * 512:20480 + (i + 1) * 512] for i in range(2)]
    oib4 = [XF[:, 21504 + i * 512:21504 + (i + 1) * 512] for i in range(2)]
    ktg4 = [XF[:, 22528 + i * 512:22528 + (i + 1) * 512] for i in range(2)]
    qf_use = [ev["gate_end"], ev["gate_end"]]; kf_use = [ev["gate_end"], ev["gate_end"]]
    ktm_use = [ev["gate_end"], ev["gate_end"]]; v_use = [ev["gate_end"], ev["gate_end"]]
    g_use = [ev["gate_end"], ev["gate_end"]]; og_use = [None, None]
    ob_use = [ev["gate_end"], ev["gate_end"]]
    qf_ld = [None, None]; ktm_ld = [None, None]; v_ld = [None, None]; g_ld = [None, None]
    pb_use = {i: None for i in range(8)}
    buf_use = {("amt", 0): ev["gate_end"], ("amt", 1): ev["gate_end"],
               ("ktg", 0): ev["gate_end"], ("ktg", 1): ev["gate_end"],
               ("oin", 0): ev["gate_end"], ("oin", 1): ev["gate_end"]}
    sbf_ev = {h: ev["SBF0"] for h in range(16)}
    su_ev = {h: ev["SBF0"] for h in range(16)}
    dmch = mb
    dm_use = [ev["qn_f"], ev["qn_f"]]
    dm_ld = [None, None]
    e_og = None
    st_og = [None, None]
    for pc in range(8):
        par = pc % 2
        mk.w(S, ev["qn_f"], qf_use[par])
        qf_ld[par] = S.emit(lambda e, pc=pc, par=par: e.dma_start(
            out=qf[par].rearrange("p (h n) -> p h n", n=256),
            in_=D_["qn_f"].ap().rearrange("(h p) t -> p h t", p=128)[
                :, :, pc * 256:(pc + 1) * 256]), mk.dq(f"qf{par}"))
        mk.w(S, ev["kn_f"], kf_use[par])
        kf_ld = S.emit(lambda e, pc=pc, par=par: e.dma_start(
            out=kf[par].rearrange("p (h n) -> p h n", n=256),
            in_=D_["kn_f"].ap().rearrange("(h p) t -> p h t", p=128)[
                :, :, pc * 256:(pc + 1) * 256]), mk.dq(f"kf{par}"))
        for half in range(2):
            tc = pc * 2 + half
            tpar = tc % 2
            toff = half * 128
            mk.w(S, ev["kn_s"], ktm_use[tpar])
            ktm_ld[tpar] = S.emit(lambda e, tc=tc, tpar=tpar: e.dma_start(
                out=ktm[tpar], in_=D_["kn_s"].ap()[tc * 128:(tc + 1) * 128, :]),
                mk.dq(f"ktm{tpar}"))
            mk.w(S, ev["v_s"], v_use[tpar])
            v_ld[tpar] = S.emit(lambda e, tc=tc, tpar=tpar: e.dma_start(
                out=vch[tpar], in_=D_["v_s"].ap()[tc * 128:(tc + 1) * 128, :]),
                mk.dq(f"vc3{tpar}"))
            mk.w(S, ev["g_s"], g_use[tpar])
            g_ld[tpar] = S.emit(lambda e, tc=tc, tpar=tpar: e.dma_start(
                out=gch[tpar], in_=D_["g_s"].ap()[tc * 128:(tc + 1) * 128, :]),
                mk.dq(f"gc3{tpar}"))
            mk.w(S, ev["dmat"], dm_use[tpar])
            dm_ld[tpar] = S.emit(lambda e, tc=tc, tpar=tpar: e.dma_start(
                out=dmch[tpar], in_=D_["dmat"].ap()[tc * 128:(tc + 1) * 128, :]),
                mk.dq(f"dm3{tpar}"))
            e_oc = None
            prev = None
            last_su = None
            for g in range(5):
                if g < 4:
                    h0 = g * 4
                    gp = g % 2
                    mmQK = None
                    for j in range(4):
                        mk.w(T, qf_ld[par], kf_ld,
                             pb_use[gp] if j == 0 else None)
                        mmQK = T.emit(lambda e, h=h0 + j, j=j, par=par, toff=toff,
                                      gp=gp: e.matmul(
                            pb[gp][:, j * 128:(j + 1) * 128],
                            kf[par][:, h * 256 + toff:h * 256 + toff + 128],
                            qf[par][:, h * 256 + toff:h * 256 + toff + 128],
                            start=True, stop=True), cT)
                    mmI = None
                    for j in range(4):
                        mk.w(T, sbf_ev[h0] if j == 0 else None,
                             pb_use[4 + gp] if j == 0 else None)
                        mmI = T.emit(lambda e, h=h0 + j, j=j, par=par, toff=toff,
                                     gp=gp: e.matmul(
                            pb[4 + gp][:, j * 128:(j + 1) * 128],
                            qf[par][:, h * 256 + toff:h * 256 + toff + 128],
                            D_["SBF"][:, h * 128:(h + 1) * 128],
                            start=True, stop=True), cT)
                    mk.w(V, mmQK, dm_ld[tpar], buf_use.get(("amt", gp)))
                    e_am = V.emit(lambda e, h0=h0, gp=gp, tpar=tpar: e.tensor_tensor(
                        amb4[gp][:, 0:512], pb[gp][:, 0:512],
                        dmch[tpar][:, h0 * 128:(h0 + 4) * 128], ALU.mult), cV)
                    pb_use[gp] = e_am
                    dm_use[tpar] = e_am
                    e_oi = None
                    for j in range(4):
                        mk.w(S, mmI if j == 0 else None,
                             ev["dec_p"] if j == 0 else None,
                             buf_use.get(("oin", gp)) if j == 0 else None)
                        e_oi = S.emit(lambda e, h=h0 + j, j=j, tc=tc, gp=gp: e.activation(
                            oib4[gp][:, j * 128:(j + 1) * 128],
                            pb[4 + gp][:, j * 128:(j + 1) * 128], AF.Identity,
                            scale=D_["dec_p"][:, tc * 16 + h:tc * 16 + h + 1]), cS)
                    pb_use[4 + gp] = e_oi
                    e_kt = None
                    for j in range(4):
                        mk.w(V, ktm_ld[tpar] if j == 0 else None,
                             ev["dec_g"] if j == 0 else None,
                             buf_use.get(("ktg", gp)) if j == 0 else None)
                        e_kt = V.emit(lambda e, h=h0 + j, j=j, tc=tc, tpar=tpar,
                                      gp=gp: e.tensor_scalar(
                            ktg4[gp][:, j * 128:(j + 1) * 128],
                            ktm[tpar][:, h * 128:(h + 1) * 128],
                            D_["dec_g"][:, tc * 16 + h:tc * 16 + h + 1],
                            None, ALU.mult), cV)
                    cur = (h0, gp, e_am, e_oi, e_kt)
                else:
                    cur = None
                if prev is not None:
                    ph0, pgp, p_am, p_oi, p_kt = prev
                    mmO = None
                    for j in range(4):
                        mk.w(T, p_am if j == 0 else None, v_ld[tpar],
                             pb_use[2 + pgp] if j == 0 else None)
                        mmO = T.emit(lambda e, h=ph0 + j, j=j, tpar=tpar,
                                     gp=pgp: e.matmul(
                            pb[2 + gp][:, j * 128:(j + 1) * 128],
                            amb4[gp][:, j * 128:(j + 1) * 128],
                            vch[tpar][:, h * 128:(h + 1) * 128],
                            start=True, stop=True), cT)
                    buf_use[("amt", pgp)] = mmO
                    mmU = None
                    for j in range(4):
                        mk.w(T, p_kt if j == 0 else None,
                             pb_use[6 + pgp] if j == 0 else None)
                        mmU = T.emit(lambda e, h=ph0 + j, j=j, tpar=tpar,
                                     gp=pgp: e.matmul(
                            pb[6 + gp][:, j * 128:(j + 1) * 128],
                            ktg4[gp][:, j * 128:(j + 1) * 128],
                            vch[tpar][:, h * 128:(h + 1) * 128],
                            start=True, stop=True), cT)
                    buf_use[("ktg", pgp)] = mmU
                    mk.w(V, p_oi, mmO, ob_use[tpar])
                    e_oc = V.emit(lambda e, h0=ph0, gp=pgp, tpar=tpar: e.tensor_add(
                        obf[tpar][:, h0 * 128:(h0 + 4) * 128],
                        oib4[gp][:, 0:512], pb[2 + gp][:, 0:512]), cV)
                    pb_use[2 + pgp] = e_oc
                    buf_use[("oin", pgp)] = e_oc
                    mk.w(V, mmU, ev["dec_phi"], su_ev[ph0])
                    for j in range(4):
                        last_su = e_su = V.emit(lambda e, h=ph0 + j, tc=tc,
                                                gp=pgp: e.scalar_tensor_tensor(
                            D_["SS"][:, h * 128:(h + 1) * 128],
                            D_["SS"][:, h * 128:(h + 1) * 128],
                            D_["dec_phi"][:, tc * 16 + h:tc * 16 + h + 1],
                            pb[6 + gp][:, (h % 4) * 128:((h % 4) + 1) * 128],
                            ALU.mult, ALU.add), cV)
                    pb_use[6 + pgp] = last_su
                prev = cur
            mk.w(V, last_su)
            e_sbc = V.emit(lambda e: e.tensor_copy(D_["SBF"][:, :], D_["SS"][:, :]), cV)
            for h in range(16):
                sbf_ev[h] = e_sbc
                su_ev[h] = e_sbc
            qf_use[par] = mmI
            kf_use[par] = mmQK
            ktm_use[tpar] = e_kt
            v_use[tpar] = emax(mmU, mmO)
            mk.w(V, e_oc, g_ld[tpar], og_use[tpar])
            e_og = V.emit(lambda e, tpar=tpar: e.tensor_mul(
                ogb[tpar], obf[tpar], gch[tpar]), cV)
            ob_use[tpar] = e_og
            g_use[tpar] = e_og
            mk.w(S, e_og)
            st_og[tpar] = S.emit(lambda e, tc=tc, tpar=tpar: e.dma_start(
                out=D_["og_s"].ap()[tc * 128:(tc + 1) * 128, :], in_=ogb[tpar]),
                mk.dq(f"ogst{tpar}"))
            og_use[tpar] = st_og[tpar]
    ev["og_s"] = emax(st_og[0], st_og[1])
    fence_all(mk)

    ev["og_f"] = mirror_pass(mk, D_, "og_s", "og_f", 16,
                             [CT[:, 0:2048], CT[:, 2048:4096]], ev["og_s"])
    e_wo = P.emit(lambda e: e.dma_start(
        out=XF[:, 0:32768].rearrange("p (vc d) -> p vc d", d=2048),
        in_=D_["wo"].rearrange("(vc p) d -> p vc d", p=128)), mk.dq("wo"))
    ogf = [CT[:, 0:8192], CT[:, 8192:16384]]
    sqb = XF[:, 32768:34816]
    QP = D_["QP"]
    x2b = [QP[:, i * 2048:(i + 1) * 2048] for i in range(2)]
    ogf_use = [None, None]
    x2_use = [None, None]
    x1_use = None
    sq_use = None
    sum_use = None
    pbu = {i: None for i in range(4)}
    last_x2 = None
    for grp in range(4):
        gp = grp % 2
        mk.w(S, ev["og_f"], ogf_use[gp])
        og_ld = S.emit(lambda e, grp=grp, gp=gp: e.dma_start(
            out=ogf[gp].rearrange("p (vc n) -> p vc n", n=512),
            in_=D_["og_f"].ap().rearrange("(vc p) t -> p vc t", p=128)[
                :, :, grp * 512:(grp + 1) * 512]), mk.dq(f"ogf{gp}"))
        for tcl in range(4):
            tok0 = grp * 512 + tcl * 128
            mk.w(S, x1_use)
            x1_ld = S.emit(lambda e, tok0=tok0: e.dma_start(
                out=AXF[:, 2048:4096],
                in_=D_["x_in"][HALO + tok0:HALO + tok0 + 128, :]), mk.dq("x1"))
            e_ep = None
            for ds in range(4):
                mk.w(T, og_ld, e_wo, pbu[ds])
                for vc in range(16):
                    mmW = T.emit(lambda e, vc=vc, ds=ds, tcl=tcl, gp=gp: e.matmul(
                        pb[ds][:, :], ogf[gp][:, vc * 512 + tcl * 128:vc * 512 + (tcl + 1) * 128],
                        XF[:, vc * 2048 + ds * 512:vc * 2048 + (ds + 1) * 512],
                        start=(vc == 0), stop=(vc == 15)), cT if vc == 15 else None)
                mk.w(V, mmW, x1_ld, sum_use)
                e_ep = V.emit(lambda e, ds=ds: e.tensor_add(
                    AXF[:, ds * 512:(ds + 1) * 512], pb[ds][:, :],
                    AXF[:, 2048 + ds * 512:2048 + (ds + 1) * 512]), cV)
                pbu[ds] = e_ep
            x1_use = e_ep
            ogf_use[gp] = mmW
            mk.w(S, e_ep, sq_use)
            e_sq = S.emit(lambda e: e.activation(sqb, AXF[:, 0:2048], AF.Square), cS)
            mk.w(V, e_ep)
            e_r1 = V.emit(lambda e: e.reduce_sum(
                D_["smallf"][:, 0:1], AXF[:, 0:2048].rearrange("p (a d) -> p a d", a=1),
                axis=AX.X), cV)
            mk.w(V, e_sq)
            e_r2 = V.emit(lambda e: e.reduce_sum(
                D_["smallf"][:, 1:2], sqb.rearrange("p (a d) -> p a d", a=1),
                axis=AX.X), cV)
            sq_use = e_r2
            e_rstd, e_m = ln_newton(mk, D_, e_r1, e_r2, 1.0 / D)
            mk.w(V, e_rstd, x2_use[tcl % 2])
            e_ap = V.emit(lambda e, tcl=tcl: e.tensor_scalar(
                x2b[tcl % 2], AXF[:, 0:2048], D_["smallf"][:, 32:33],
                D_["smallf"][:, 112:113], ALU.subtract, ALU.mult), cV)
            sum_use = e_ap
            mk.w(S, e_ap)
            e_st = S.emit(lambda e, tok0=tok0, tcl=tcl: e.dma_start(
                out=D_["x2_s"].ap()[tok0:tok0 + 128, :], in_=x2b[tcl % 2]),
                mk.dq(f"x2st{tcl % 2}"))
            x2_use[tcl % 2] = e_st
            last_x2 = e_st
    ev["x2_s"] = last_x2
    fence_all(mk)
    return ev


def ln_newton(mk, D_, e_sum, e_sumsq, inv_n):
    V, cV = mk.V, mk.cV
    smallf, smalli = D_["smallf"], D_["smalli"]
    mk.w(V, e_sum)
    e_m = V.emit(lambda e: e.tensor_scalar_mul(smallf[:, 32:33], smallf[:, 0:1], inv_n), cV)
    mk.w(V, e_sumsq)
    e_x2 = V.emit(lambda e: e.tensor_scalar_mul(smallf[:, 48:49], smallf[:, 1:2], inv_n), cV)
    mk.w(V, e_m)
    e_m2 = V.emit(lambda e: e.tensor_mul(smallf[:, 64:65], smallf[:, 32:33],
                                         smallf[:, 32:33]), cV)
    mk.w(V, e_m2, e_x2)
    e_var = V.emit(lambda e: e.tensor_sub(smallf[:, 80:81], smallf[:, 48:49],
                                          smallf[:, 64:65]), cV)
    mk.w(V, e_var)
    e_vr = V.emit(lambda e: e.tensor_scalar_add(smallf[:, 80:81], smallf[:, 80:81],
                                                LNEPS), cV)
    mk.w(V, e_vr, D_["__magic_ev"])
    e_i1 = V.emit(lambda e: e.tensor_scalar(smalli[:, 0:1], smallf[:, 80:81].bitcast(I32),
                                            1, None, ALU.arith_shift_right), cV)
    mk.w(V, e_i1)
    e_prev = V.emit(lambda e: e.tensor_sub(smalli[:, 32:33], smalli[:, 16:17],
                                           smalli[:, 0:1]), cV)
    ycur = smalli[:, 32:33].bitcast(F32)
    for itn in range(2):
        mk.w(V, e_prev)
        e_y2 = V.emit(lambda e, ycur=ycur: e.tensor_mul(smallf[:, 96:97], ycur, ycur), cV)
        mk.w(V, e_y2)
        e_t = V.emit(lambda e: e.tensor_mul(smallf[:, 96:97], smallf[:, 96:97],
                                            smallf[:, 80:81]), cV)
        mk.w(V, e_t)
        e_t2 = V.emit(lambda e: e.tensor_scalar(smallf[:, 96:97], smallf[:, 96:97],
                                                -0.5, 1.5, ALU.mult, ALU.add), cV)
        mk.w(V, e_t2)
        e_prev = V.emit(lambda e, ycur=ycur: e.tensor_mul(smallf[:, 112:113], ycur,
                                                          smallf[:, 96:97]), cV)
        ycur = smallf[:, 112:113]
    return e_prev, e_m


def phase4(mk, D_, ev, ev_w2f):
    g = lambda k: D_[k]
    T, V, S, P, Dq = mk.T, mk.V, mk.S, mk.P, mk.Dq
    cT, cV, cS, cP, dD, dS, dP = mk.cT, mk.cV, mk.cS, mk.cP, mk.dD, mk.dS, mk.dP
    XF, CT, WS, QP, AXF, pb = g("XF"), g("CT"), g("WS"), g("QP"), g("AXF"), g("pb")
    x2f = CT[:, 0:8192]
    sums = [CT[:, 8192 + i * 2048:8192 + (i + 1) * 2048] for i in range(4)]
    w2tile = [CT[:, 16384 + i * 1024:16384 + (i + 1) * 1024] for i in range(2)]
    x2tm = [CT[:, 18432 + i * 512:18432 + (i + 1) * 512] for i in range(2)]
    sil = [CT[:, 19456 + i * 512:19456 + (i + 1) * 512] for i in range(2)]
    sqb = XF[:, 32768:34816]
    outst = AXF[:, 0:2048]
    w1buf = [WS[:, 0:4096], WS[:, 4096:8192]]
    w3buf = [QP[:, 0:4096], QP[:, 4096:8192]]
    x2f_use = None
    w1_use = [None, None]; w3_use = [None, None]
    w2_use = [None, None]
    x2tm_use = [None, None]
    sil_use = [None, None]
    gprev = None
    sums_use = [None] * 4
    pbu = {i: None for i in range(8)}
    sq_use = None
    out_use = None
    last_out = None
    ev["x2_f"] = mirror_pass(mk, D_, "x2_s", "x2_f", 16,
                             [CT[:, 0:2048], CT[:, 2048:4096]], ev["x2_s"])
    pref = {}
    for grp in range(4):
        if grp in pref:
            x2_ld, w1_ld0, w3_ld0 = pref.pop(grp)
        else:
            mk.w(S, ev["x2_f"], x2f_use)
            x2_ld = S.emit(lambda e, grp=grp: e.dma_start(
                out=x2f.rearrange("p (dc n) -> p dc n", n=512),
                in_=D_["x2_f"].ap().rearrange("(dc p) t -> p dc t", p=128)[
                    :, :, grp * 512:(grp + 1) * 512]), mk.dq("x2fl"))
            w1_ld0 = w3_ld0 = None
        e_gt_last = None
        for pr in range(32):
            wp = pr % 2
            if pr == 0 and w1_ld0 is not None:
                w1_ld, w3_ld = w1_ld0, w3_ld0
            else:
                mk.w(S, ev["w1t_f"], w1_use[wp])
                w1_ld = S.emit(lambda e, pr=pr, wp=wp: e.dma_start(
                    out=w1buf[wp].rearrange("p (dc n) -> p dc n", n=256),
                    in_=D_["w1t_f"].ap().rearrange("(dc p) t -> p dc t", p=128)[
                        :, :, pr * 256:(pr + 1) * 256]), mk.dq(f"w1l{wp}"))
                mk.w(S, ev["w3t_f"], w3_use[wp])
                w3_ld = S.emit(lambda e, pr=pr, wp=wp: e.dma_start(
                    out=w3buf[wp].rearrange("p (dc n) -> p dc n", n=256),
                    in_=D_["w3t_f"].ap().rearrange("(dc p) t -> p dc t", p=128)[
                        :, :, pr * 256:(pr + 1) * 256]), mk.dq(f"w3l{wp}"))
            for hl in range(2):
                hs = pr * 2 + hl
                bk = hs % 2
                mk.w(T, x2_ld, w1_ld, pbu[bk], gprev)
                for dc in range(16):
                    mmUp = T.emit(lambda e, dc=dc, hl=hl, bk=bk, wp=wp: e.matmul(
                        pb[bk][:, :], w1buf[wp][:, dc * 256 + hl * 128:dc * 256 + (hl + 1) * 128],
                        x2f[:, dc * 512:(dc + 1) * 512],
                        start=(dc == 0), stop=(dc == 15)), cT if dc == 15 else None)
                w1_use[wp] = mmUp
                mk.w(T, w3_ld, pbu[2 + bk])
                for dc in range(16):
                    mmL = T.emit(lambda e, dc=dc, hl=hl, bk=bk, wp=wp: e.matmul(
                        pb[2 + bk][:, :], w3buf[wp][:, dc * 256 + hl * 128:dc * 256 + (hl + 1) * 128],
                        x2f[:, dc * 512:(dc + 1) * 512],
                        start=(dc == 0), stop=(dc == 15)), cT if dc == 15 else None)
                w3_use[wp] = mmL
                mk.w(S, mmUp, sil_use[bk])
                e_si = S.emit(lambda e, bk=bk: e.activation(
                    sil[bk], pb[bk][:, :], AF.Silu), cS)
                pbu[bk] = e_si
                mk.w(V, e_si, mmL, gprev)
                e_gt = V.emit(lambda e, hs=hs, bk=bk: e.tensor_tensor(
                    XF[:, hs * 512:(hs + 1) * 512], sil[bk], pb[2 + bk][:, :],
                    ALU.mult), cV)
                pbu[2 + bk] = e_gt
                sil_use[bk] = e_gt
                e_gt_last = e_gt
        x2f_use = mmL
        if grp < 3:
            mk.w(Dq, x2f_use)
            nx_x2 = Dq.emit(lambda e, grp=grp: e.dma_start(
                out=x2f.rearrange("p (dc n) -> p dc n", n=512),
                in_=D_["x2_f"].ap().rearrange("(dc p) t -> p dc t", p=128)[
                    :, :, (grp + 1) * 512:(grp + 2) * 512]), mk.dq("x2fl"))
            mk.w(Dq, w1_use[0])
            nx_w1 = Dq.emit(lambda e: e.dma_start(
                out=w1buf[0].rearrange("p (dc n) -> p dc n", n=256),
                in_=D_["w1t_f"].ap().rearrange("(dc p) t -> p dc t", p=128)[
                    :, :, 0:256]), mk.dq("w1l0"))
            mk.w(Dq, w3_use[0])
            nx_w3 = Dq.emit(lambda e: e.dma_start(
                out=w3buf[0].rearrange("p (dc n) -> p dc n", n=256),
                in_=D_["w3t_f"].ap().rearrange("(dc p) t -> p dc t", p=128)[
                    :, :, 0:256]), mk.dq("w3l0"))
            pref[grp + 1] = (nx_x2, nx_w1, nx_w3)
        mmD_ = None
        for ds in range(4):
            for hg in range(32):
                wp = hg % 2
                mk.w(S, ev_w2f, w2_use[wp])
                w2_ld = S.emit(lambda e, hg=hg, ds=ds, wp=wp: e.dma_start(
                    out=w2tile[wp].rearrange("p (c n) -> p c n", n=512),
                    in_=D_["w2f"].ap().rearrange("(c p) n -> p c n", p=128)[
                        :, hg * 2:(hg + 1) * 2, ds * 512:(ds + 1) * 512]),
                    mk.dq(f"w2t{wp}"))
                for hs in range(2):
                    hc = hg * 2 + hs
                    for tcl in range(4):
                        bk = 7 - tcl
                        mk.w(T, w2_ld, e_gt_last)
                        if hc == 0:
                            mk.w(T, pbu[bk])
                        mmD_ = T.emit(lambda e, hc=hc, tcl=tcl, bk=bk, wp=wp, hs=hs: e.matmul(
                            pb[bk][:, :], XF[:, hc * 512 + tcl * 128:hc * 512 + (tcl + 1) * 128],
                            w2tile[wp][:, hs * 512:(hs + 1) * 512],
                            start=(hc == 0), stop=(hc == 63)),
                            cT if (hc == 63 or tcl == 3) else None)
                w2_use[wp] = mmD_
            for tcl in range(4):
                bk = 7 - tcl
                tok0 = grp * 512 + tcl * 128
                mk.w(S, ev["x2_s"], x2tm_use[tcl % 2])
                x2t_ld = S.emit(lambda e, tok0=tok0, ds=ds, tcl=tcl: e.dma_start(
                    out=x2tm[tcl % 2], in_=D_["x2_s"].ap()[tok0:tok0 + 128,
                                                           ds * 512:(ds + 1) * 512]),
                    mk.dq(f"x2tm{tcl % 2}"))
                mk.w(V, mmD_, x2t_ld, sums_use[tcl])
                e_ep = V.emit(lambda e, tcl=tcl, ds=ds, bk=bk: e.tensor_add(
                    sums[tcl][:, ds * 512:(ds + 1) * 512], pb[bk][:, :],
                    x2tm[tcl % 2]), cV)
                pbu[bk] = e_ep
                x2tm_use[tcl % 2] = e_ep
                if DBG_P4 == grp and tcl == 2:
                    mk.w(V, e_ep)
                    V.emit(lambda e, ds=ds: e.tensor_copy(
                        D_["dbgf"][:, ds * 256:(ds + 1) * 256], pb[6][:, 0:256]), None)
                    e_cap = V.emit(lambda e, ds=ds, tcl=tcl: e.tensor_copy(
                        D_["dbgb"][:, ds * 256:(ds + 1) * 256], x2tm[tcl % 2][:, 0:256]), cV)
                    pbu[bk] = e_cap
                    x2tm_use[tcl % 2] = e_cap
        gprev = mmD_
        for tcl in range(4):
            bk2 = 7 - tcl
            tok0 = grp * 512 + tcl * 128
            mk.w(S, pbu[bk2], sq_use)
            e_sq = S.emit(lambda e, tcl=tcl: e.activation(
                sqb, sums[tcl], AF.Square), cS)
            mk.w(V, pbu[bk2])
            e_r1 = V.emit(lambda e, tcl=tcl: e.reduce_sum(
                D_["smallf"][:, 0:1], sums[tcl].rearrange("p (a d) -> p a d", a=1),
                axis=AX.X), cV)
            mk.w(V, e_sq)
            e_r2 = V.emit(lambda e: e.reduce_sum(
                D_["smallf"][:, 1:2], sqb.rearrange("p (a d) -> p a d", a=1),
                axis=AX.X), cV)
            sq_use = e_r2
            e_rstd, e_m = ln_newton(mk, D_, e_r1, e_r2, 1.0 / D)
            mk.w(V, e_rstd, out_use)
            e_ap = V.emit(lambda e, tcl=tcl: e.tensor_scalar(
                outst, sums[tcl], D_["smallf"][:, 32:33], D_["smallf"][:, 112:113],
                ALU.subtract, ALU.mult), cV)
            sums_use[tcl] = e_ap
            pbu[bk2] = e_ap
            mk.w(S, e_ap)
            last_out = S.emit(lambda e, tok0=tok0: e.dma_start(
                out=D_["out"][tok0:tok0 + 128, :], in_=outst), mk.dq("outst"))
            out_use = last_out
    mk.w(Dq, last_out)
    mk.w(S, last_out)
    return ev


def emit_debug_taps(mk, D_):
    if not DEBUG_TAPS:
        return
    F = fence_all(mk)
    prev = None
    for name in DEBUG_TAPS:
        t = D_[name]
        ap = t.ap() if hasattr(t, "ap") else t[:, :]
        shp = [int(s) for s in ap.shape]
        dt = ap.dtype
        o = mk.dout("dbg_" + name, shp, dt)
        mk.w(mk.P, prev)
        prev = mk.P.emit(lambda e, o=o, ap=ap: e.dma_start(out=o, in_=ap), mk.dq("dbg"))
    mk.w(mk.P, prev)


_ONES = np.ones((128, 128), dtype=np.float32)
_j = np.arange(128)
_UL = (_j[:, None] <= _j[None, :]).astype(np.float32)


def _consts():
    h128 = (hadamard(128) / np.sqrt(2.0)).astype(ml_dtypes.bfloat16)
    h64 = (hadamard(64) / 64.0).astype(ml_dtypes.bfloat16)
    tri = _UL.astype(ml_dtypes.bfloat16)
    return h128, h64, tri


def kernel(**inputs):
    pre = "L0_attn_"
    x = np.asarray(inputs[pre + "hidden_states"], dtype=np.float32)
    h128, h64, tri = _consts()
    base = dict(
        wq=np.ascontiguousarray(inputs[pre + "Wq"], dtype=np.float32),
        wk=np.ascontiguousarray(inputs[pre + "Wk"], dtype=np.float32),
        wv=np.ascontiguousarray(inputs[pre + "Wv"], dtype=np.float32),
        wg=np.ascontiguousarray(inputs[pre + "Wg"], dtype=np.float32),
        wo=np.ascontiguousarray(inputs[pre + "Wo"], dtype=np.float32),
        wb=np.ascontiguousarray(inputs[pre + "Wb"], dtype=np.float32),
        bb=np.ascontiguousarray(inputs[pre + "bb"], dtype=np.float32),
        qcw=np.ascontiguousarray(inputs[pre + "qconv_w"], dtype=np.float32),
        qcb=np.ascontiguousarray(inputs[pre + "qconv_b"], dtype=np.float32),
        kcw=np.ascontiguousarray(inputs[pre + "kconv_w"], dtype=np.float32),
        kcb=np.ascontiguousarray(inputs[pre + "kconv_b"], dtype=np.float32),
        qnw=np.ascontiguousarray(inputs[pre + "qn_w"], dtype=np.float32),
        qnb=np.ascontiguousarray(inputs[pre + "qn_b"], dtype=np.float32),
        knw=np.ascontiguousarray(inputs[pre + "kn_w"], dtype=np.float32),
        knb=np.ascontiguousarray(inputs[pre + "kn_b"], dtype=np.float32),
        w1=np.ascontiguousarray(inputs["L0_ffn_W1"], dtype=np.float32),
        w3=np.ascontiguousarray(inputs["L0_ffn_W3"], dtype=np.float32),
        w2=np.ascontiguousarray(inputs["L0_ffn_W2"], dtype=np.float32),
        c_ones=_ONES, c_ul=_UL, c_tri=tri, c_h128=h128, c_h64=h64,
        c_eye=np.eye(128, dtype=np.float32),
    )
    for k in ("bb",):
        base[k] = base[k].reshape(1, H)
    for k in ("qnw", "qnb", "knw", "knb"):
        base[k] = base[k].reshape(1, DKH)
    in_maps = []
    for c in range(NCORE):
        b, s0 = c // 4, (c % 4) * TPC
        xs = np.zeros((TPAD, D), dtype=np.float32)
        lo = max(0, s0 - HALO)
        xs[HALO - (s0 - lo):] = x[b, lo:s0 + TPC]
        pm = np.zeros(8, dtype=np.float32)
        for c2 in range(8):
            if c2 // 4 == b and c2 < c:
                pm[c2] = 1.0
        m = dict(base)
        m["x"] = xs
        m["c_pm"] = pm.reshape(1, 8)
        in_maps.append(m)

    nc = build_program()
    nc.has_collectives = True
    res = run_bass_kernel_spmd(nc, in_maps, core_ids=list(range(NCORE)))
    global _LAST_RESULTS
    _LAST_RESULTS = res.results
    y = np.concatenate([res.results[c]["out"] for c in range(NCORE)], axis=0)
    return y.reshape(B, L, D)


_LAST_RESULTS = None


if __name__ == "__main__":
    nc = build_program()
    print("build ok")


# version 27
# speedup vs baseline: 1.0378x; 1.0378x over previous
import numpy as np
import ml_dtypes
from contextlib import ExitStack

import concourse.bass as bass
import concourse.mybir as mybir
from concourse.bass_utils import run_bass_kernel_spmd

F32 = mybir.dt.float32
BF16 = mybir.dt.bfloat16
I32 = mybir.dt.int32
AF = mybir.ActivationFunctionType
ALU = mybir.AluOpType
AX = mybir.AxisListType

B, L, D = 2, 8192, 2048
H, DKH, DVH = 16, 128, 128
FH = 8192
NCORE = 8
TPC = 2048
HALO = 128
TPAD = TPC + HALO
CH = 128
NCH = TPC // CH
NSL = 16
LNEPS = 1e-5
CLAMP = 1e-6

DEBUG_TAPS = []
DBG_CELL = None
SCAN_FENCE = False
DBG_P4 = None


class Ctr:
    def __init__(self, sem, step):
        self.sem, self.step, self.n = sem, step, 0


class St:
    def __init__(self, key):
        self.key = key
        self.ops = []
        self.waited = {}

    def emit(self, fn, inc=None):
        if inc is None:
            self.ops.append(("op", fn))
            return None
        inc.n += inc.step
        self.ops.append(("inc", fn, inc.sem, inc.step))
        return (inc, inc.n)

    def wait(self, ev):
        if ev is None:
            return
        ctr, v = ev
        if self.waited.get(id(ctr), 0) >= v:
            return
        self.waited[id(ctr)] = v
        self.ops.append(("wait", ctr.sem, v))

    def replay(self, e):
        for it in self.ops:
            if it[0] == "op":
                it[1](e)
            elif it[0] == "inc":
                it[1](e).then_inc(it[2], it[3])
            else:
                e.wait_ge(it[1], it[2])


def emax(*evs):
    out = {}
    for ev in evs:
        if ev is None:
            continue
        c, v = ev
        if out.get(id(c), (None, -1))[1] < v:
            out[id(c)] = (c, v)
    return list(out.values())


class MK:
    def __init__(self):
        self.nc = bass.Bass("TRN2", target_bir_lowering=False)
        self.es = ExitStack()
        self.T = St("T"); self.V = St("V"); self.S = St("S")
        self.P = St("P"); self.Dq = St("D")
        nc = self.nc
        sem = lambda n: self.es.enter_context(nc.semaphore(n))
        self.cT = Ctr(sem("cT"), 1)
        self.cV = Ctr(sem("cV"), 1)
        self.cS = Ctr(sem("cS"), 1)
        self.cP = Ctr(sem("cP"), 1)
        self.dD = Ctr(sem("dD"), 16)
        self.dS = Ctr(sem("dS"), 16)
        self.dP = Ctr(sem("dP"), 16)
        self.cc = Ctr(sem("ccs"), 1)
        self.dbg_out = {}
        self._dmas = {}

    def dq(self, name):
        if name not in self._dmas:
            s = self.es.enter_context(self.nc.semaphore("dma_" + name))
            self._dmas[name] = Ctr(s, 16)
        return self._dmas[name]

    def din(self, name, shape, dt=F32):
        return self.nc.dram_tensor(name, shape, dt, kind="ExternalInput").ap()

    def dout(self, name, shape, dt=F32):
        return self.nc.dram_tensor(name, shape, dt, kind="ExternalOutput").ap()

    def dscr(self, name, shape, dt):
        return self.nc.dram_tensor(name, shape, dt, kind="Internal")

    def sb(self, name, shape, dt):
        return self.es.enter_context(self.nc.sbuf_tensor(name, shape, dt))

    def ps(self, name, shape):
        return self.es.enter_context(self.nc.psum_tensor(name, shape, F32))

    def w(self, stream, *evs):
        for ev in evs:
            if ev is None:
                continue
            if isinstance(ev, list):
                self.w(stream, *ev)
            else:
                stream.wait(ev)


def hadamard(n):
    h = np.array([[1.0]], dtype=np.float64)
    while h.shape[0] < n:
        h = np.block([[h, h], [h, -h]])
    return h


def build_program():
    mk = MK()
    nc, es = mk.nc, mk.es
    T, V, S, P, Dq = mk.T, mk.V, mk.S, mk.P, mk.Dq
    cT, cV, cS, cP, dD, dS, dP = mk.cT, mk.cV, mk.cS, mk.cP, mk.dD, mk.dS, mk.dP

    x_in = mk.din("x", [TPAD, D])
    wq = mk.din("wq", [D, D]); wk = mk.din("wk", [D, D]); wv = mk.din("wv", [D, D])
    wg = mk.din("wg", [D, D]); wo = mk.din("wo", [D, D])
    wb = mk.din("wb", [D, H]); bb = mk.din("bb", [1, H])
    qcw = mk.din("qcw", [D, 4]); qcb = mk.din("qcb", [D])
    kcw = mk.din("kcw", [D, 4]); kcb = mk.din("kcb", [D])
    qnw = mk.din("qnw", [1, DKH]); qnb = mk.din("qnb", [1, DKH])
    knw = mk.din("knw", [1, DKH]); knb = mk.din("knb", [1, DKH])
    w1 = mk.din("w1", [FH, D]); w3 = mk.din("w3", [FH, D]); w2 = mk.din("w2", [D, FH])
    c_ones = mk.din("c_ones", [128, 128])
    c_ul = mk.din("c_ul", [128, 128])
    c_tri = mk.din("c_tri", [128, 128], BF16)
    c_h128 = mk.din("c_h128", [128, 128], BF16)
    c_h64 = mk.din("c_h64", [64, 64], BF16)
    c_pm = mk.din("c_pm", [1, 8])
    c_eye = mk.din("c_eye", [128, 128])

    out = mk.dout("out", [TPC, D])

    xbf = mk.dscr("xbf", [TPAD, D], BF16)
    wqb = mk.dscr("wqb", [D, D], BF16); wkb = mk.dscr("wkb", [D, D], BF16)
    wvb = mk.dscr("wvb", [D, D], BF16); wgb = mk.dscr("wgb", [D, D], BF16)
    wob = mk.dscr("wob", [D, D], BF16); wbb = mk.dscr("wbb", [D, H], BF16)
    w1b = mk.dscr("w1b", [FH, D], BF16); w3b = mk.dscr("w3b", [FH, D], BF16)
    w2b = mk.dscr("w2b", [D, FH], BF16)
    w2t = mk.dscr("w2t", [FH, D], BF16)
    w2f = mk.dscr("w2f", [FH, D], BF16)
    qpre = mk.dscr("qpre", [D, TPC], BF16)
    kpre = mk.dscr("kpre", [D, TPC], BF16)
    qn_s = mk.dscr("qn_s", [TPC, D], BF16)
    kn_s = mk.dscr("kn_s", [TPC, D], BF16)
    v_s = mk.dscr("v_s", [TPC, D], BF16)
    g_s = mk.dscr("g_s", [TPC, D], BF16)
    og_s = mk.dscr("og_s", [TPC, D], BF16)
    x2_s = mk.dscr("x2_s", [TPC, D], BF16)
    qn_f = mk.dscr("qn_f", [D, TPC], BF16)
    kn_f = mk.dscr("kn_f", [D, TPC], BF16)
    og_f = mk.dscr("og_f", [D, TPC], BF16)
    x2_f = mk.dscr("x2_f", [D, TPC], BF16)
    dmat = mk.dscr("dmat", [TPC, D], BF16)
    w1t_f = mk.dscr("w1t_f", [D, FH], BF16)
    w3t_f = mk.dscr("w3t_f", [D, FH], BF16)
    csum_in = mk.dscr("csum_in", [17, DKH * DVH], F32)
    csum_out = mk.dscr("csum_out", [8 * 17, DKH * DVH], F32)

    XF = mk.sb("XF", [128, 16 * TPAD], BF16)
    CT = mk.sb("CT", [128, 20480], BF16)
    WS = mk.sb("WS", [128, 8192], BF16)
    QP = mk.sb("QP", [128, 2 * TPAD + 2 * TPC], BF16)
    AXF = mk.sb("AXF", [128, 4096], F32)
    beta_cm = mk.sb("beta_cm", [128, 256], F32)
    l_cm = mk.sb("l_cm", [128, 256], F32)
    lnb_cm = mk.sb("lnb_cm", [128, 256], F32)
    dec_ncl = mk.sb("dec_ncl", [128, 256], F32)
    dec_g = mk.sb("dec_g", [128, 256], F32)
    dec_phi = mk.sb("dec_phi", [128, 256], F32)
    dec_p = mk.sb("dec_p", [128, 256], F32)
    dec_cum = mk.sb("dec_cum", [128, 256], F32)
    dec_slr = mk.sb("dec_slr", [128, 256], F32)
    phitot = mk.sb("phitot", [128, 16], F32)
    uhat = mk.sb("uhat", [128, 2048], F32)
    SS = mk.sb("SS", [128, 2048], F32)
    SBF = mk.sb("SBF", [128, 2048], BF16)
    ones_f = mk.sb("ones_f", [128, 128], F32)
    ul_f = mk.sb("ul_f", [128, 128], F32)
    tri_b = mk.sb("tri_b", [128, 128], BF16)
    mneg = mk.sb("mneg", [128, 128], F32)
    h128_b = mk.sb("h128_b", [128, 128], BF16)
    h64_b = mk.sb("h64_b", [64, 64], BF16)
    cwq = mk.sb("cwq", [128, 64], F32)
    cwk = mk.sb("cwk", [128, 64], F32)
    cbq = mk.sb("cbq", [128, 16], F32)
    cbk = mk.sb("cbk", [128, 16], F32)
    bbr = mk.sb("bbr", [128, 16], F32)
    gam = mk.sb("gam", [128, 128], BF16)
    bet = mk.sb("bet", [128, 128], BF16)
    smallf = mk.sb("smallf", [128, 192], F32)
    smalli = mk.sb("smalli", [128, 64], I32)
    rowf = mk.sb("rowf", [1, 512], F32)
    eye_f = mk.sb("eye_f", [128, 128], F32)
    eye_b = mk.sb("eye_b", [128, 128], BF16)
    pm_r = mk.sb("pm_r", [128, 16], F32)
    phig = mk.sb("phig", [128, 128], F32)
    wcomp = mk.sb("wcomp", [128, 160], F32)
    uldb = mk.sb("uldb", [128, 256], F32)
    dbgf = mk.sb("dbgf", [128, 512], F32)
    dbgb = mk.sb("dbgb", [128, 512], BF16)
    drawb = mk.sb("drawb", [128, 256], F32)
    dmatb = mk.sb("dmatb", [128, 256], BF16)
    qkmb = mk.sb("qkmb", [128, 256], BF16)
    amatb = mk.sb("amatb", [128, 256], BF16)
    ktgb = mk.sb("ktgb", [128, 256], BF16)
    ointb = mk.sb("ointb", [128, 256], F32)

    pb = [mk.ps(f"pb{i}", [128, 512]) for i in range(8)]

    D_ = {}
    D_.update(locals())

    ev = phase0(mk, D_)
    D_["__magic_ev"] = None
    ev = phase1(mk, D_, ev)
    ev = phase3(mk, D_, ev)
    fence_all(mk)
    phase4(mk, D_, ev, ev["w2f_done"])
    emit_debug_taps(mk, D_)

    with mk.es, nc.Block() as block:
        @block.tensor
        def _(e):
            mk.T.replay(e)

        @block.vector
        def _(e):
            mk.V.replay(e)

        @block.scalar
        def _(e):
            mk.S.replay(e)

        @block.gpsimd
        def _(e):
            mk.P.replay(e)

        @block.sync
        def _(e):
            mk.Dq.replay(e)

    n_inst = sum(len(s.ops) for s in (mk.T, mk.V, mk.S, mk.P, mk.Dq))
    print(f"[kernel build] ~{n_inst} stream items")
    return nc


def phase0(mk, D_):
    g = lambda k: D_[k]
    P, Dq, T, V, S = mk.P, mk.Dq, mk.T, mk.V, mk.S
    dP, dD, cT, cV = mk.dP, mk.dD, mk.cT, mk.cV
    ev = {}

    xs, xd = g("x_in"), g("xbf")
    prev = None
    for j in range(4):
        mk.w(P, prev)
        ev[f"xbf{j}"] = P.emit(lambda e, j=j: e.dma_start(
            out=xd[:, j * 512:(j + 1) * 512], in_=xs[:, j * 512:(j + 1) * 512]),
            mk.dq("cast0"))
        prev = ev[f"xbf{j}"]
    ev["xbf"] = prev
    ev["__cast0"] = ev["xbf"]

    cns_names = []
    for dst, src in [("ones_f", "c_ones"), ("ul_f", "c_ul"), ("tri_b", "c_tri"),
                     ("h128_b", "c_h128"), ("h64_b", "c_h64"), ("eye_f", "c_eye")]:
        dt, st_ = g(dst), g(src)
        ev[dst] = S.emit(lambda e, a=dt, b=st_: e.dma_start(out=a[:], in_=b),
                         mk.dq("cns"))
        cns_names.append(dst)
    for dst, src in [("cwq", "qcw"), ("cwk", "kcw")]:
        dt, st_ = g(dst), g(src)
        ev[dst] = S.emit(
            lambda e, a=dt, b=st_: e.dma_start(
                out=a[:, :], in_=b.rearrange("(h dk) j -> dk h j", dk=128)),
            mk.dq("cns"))
        cns_names.append(dst)
    for dst in cns_names:
        ev[dst] = (mk.dq("cns"), mk.dq("cns").n)
    rowf, smallf = g("rowf"), g("smallf")
    prev_use = None
    for dst, src in [("cbq", "qcb"), ("cbk", "kcb")]:
        st_ = g(src)
        mk.w(S, prev_use)
        evl = S.emit(lambda e, b=st_: e.dma_start(
            out=smallf[0:16, 0:128], in_=b.rearrange("(h dk) -> h dk", dk=128)),
            mk.dq("cbl"))
        mk.w(T, evl, ev["eye_f"])
        evm = T.emit(lambda e: e.matmul(D_["pb"][7][:, 0:16], smallf[0:16, 0:128],
                                        D_["eye_f"][0:16, 0:16], start=True, stop=True), cT)
        prev_use = evm
        mk.w(V, evm)
        evc = V.emit(lambda e, a=g(dst): e.tensor_copy(a[:, :], D_["pb"][7][:, 0:16]), cV)
        ev[dst] = evc
        mk.w(T, evc)
    evl = S.emit(lambda e: e.dma_start(out=rowf[0:1, 256:272], in_=D_["bb"]),
                 mk.dq("bbrow"))
    mk.w(T, evl, ev["ones_f"])
    evm = T.emit(lambda e: e.matmul(D_["pb"][7][:, 16:32], D_["ones_f"][0:1, :],
                                    rowf[0:1, 256:272], start=True, stop=True), cT)
    mk.w(V, evm)
    ev["bbr"] = V.emit(lambda e: e.tensor_copy(D_["bbr"][:, :], D_["pb"][7][:, 16:32]), cV)
    mk.w(T, ev["bbr"])
    mk.w(V, ev["eye_f"])
    ev["eye_b"] = V.emit(lambda e: e.tensor_copy(
        D_["eye_b"][:, :], D_["eye_f"][:, :]), cV)
    mk.w(V, ev["tri_b"])
    ev["mneg"] = V.emit(lambda e: e.tensor_scalar(
        D_["mneg"][:, :], D_["tri_b"][:], 100.0, -100.0, ALU.mult, ALU.add), cV)
    return ev


def dmat_pass(mk, D_, ev):
    T, V, S, P = mk.T, mk.V, mk.S, mk.P
    cT, cV, cS = mk.cT, mk.cV, mk.cS
    pb = D_["pb"]
    uldc = [D_["uldb"], D_["drawb"]]
    dtile = [D_["dmatb"][:, 0:128], D_["dmatb"][:, 128:256],
             D_["qkmb"][:, 0:128], D_["qkmb"][:, 128:256]]
    ftile = [D_["dbgf"][:, t * 128:(t + 1) * 128] for t in range(4)]
    dt_st = [None] * 4
    dt_ex = [None] * 4
    mmD2 = [None, None]
    pb_d = [None, None]
    dm_last = []
    for tc in range(16):
        for h2 in range(8):
            p2 = h2 % 2
            eu = []
            mk.w(V, ev["dec_l"], ev["ul_f"], mmD2[p2])
            for hh in range(2):
                h = h2 * 2 + hh
                eu.append(V.emit(lambda e, tc=tc, h=h, hh=hh, p2=p2: e.tensor_scalar(
                    uldc[p2][:, hh * 128:(hh + 1) * 128], D_["ul_f"][:],
                    D_["l_cm"][:, tc * 16 + h:tc * 16 + h + 1], None, ALU.mult), cV))
            mk.w(T, eu, ev["ones_f"], pb_d[p2])
            mm = T.emit(lambda e, p2=p2: e.matmul(
                pb[6 + p2][:, 0:256], D_["ones_f"][:], uldc[p2][:, 0:256],
                start=True, stop=True), cT)
            mmD2[p2] = mm
            newst = []
            for hh in range(2):
                h = h2 * 2 + hh
                t = h % 4
                mk.w(V, mm, ev["mneg"], ev["dec_ncl"], dt_ex[t])
                e_st = V.emit(lambda e, tc=tc, h=h, hh=hh, p2=p2, t=t: e.scalar_tensor_tensor(
                    ftile[t], pb[6 + p2][:, hh * 128:(hh + 1) * 128],
                    D_["dec_ncl"][:, tc * 16 + h:tc * 16 + h + 1], D_["mneg"][:, :],
                    ALU.add, ALU.add), cV)
                newst.append(e_st)
                mk.w(S, e_st, dt_st[t])
                e_ex = S.emit(lambda e, t=t: e.activation(
                    dtile[t], ftile[t], AF.Exp), cS)
                dt_ex[t] = e_ex
                mk.w(P, e_ex)
                e_sv = P.emit(lambda e, tc=tc, h=h, t=t: e.dma_start(
                    out=D_["dmat"].ap()[tc * 128:(tc + 1) * 128, h * 128:(h + 1) * 128],
                    in_=dtile[t]), mk.dq(f"dms{t}"))
                dt_st[t] = e_sv
                if tc == 15 and h >= 12:
                    dm_last.append(e_sv)
            pb_d[p2] = newst
    ev["dmat"] = dm_last
    return ev


def w2f_fold(mk, D_, ev, gate_ev):
    P, Dq, S, T, V = mk.P, mk.Dq, mk.S, mk.T, mk.V
    dP, dD, dS, cT, cV = mk.dP, mk.dD, mk.dS, mk.cT, mk.cV
    CT = D_["QP"]
    t_ev = [gate_ev, gate_ev]
    s_ev = [None, None]
    mm_ev = [None, None]
    it = 0
    for hc in range(64):
        for dg in range(4):
            par = it % 2
            tin = CT[:, par * 512:(par + 1) * 512]
            tout = CT[:, 1024 + par * 512:1024 + (par + 1) * 512]
            mk.w(Dq, ev["w2b"], mm_ev[par], gate_ev)
            ld = Dq.emit(lambda e, tin=tin, hc=hc, dg=dg: e.dma_start(
                out=tin, in_=D_["w2b"].ap()[dg * 512:(dg + 1) * 512,
                                            hc * 128:(hc + 1) * 128],
                transpose=True), mk.dq(f"w2f_l{par}"))
            mk.w(T, ld, ev["h128_b"], t_ev[par])
            evm = T.emit(lambda e, tin=tin, par=par: e.matmul(
                D_["pb"][par][:, :], D_["h128_b"][:], tin, start=True, stop=True), cT)
            mm_ev[par] = evm
            mk.w(V, evm, s_ev[par])
            evc = V.emit(lambda e, tout=tout, par=par: e.tensor_copy(
                tout, D_["pb"][par][:, :]), cV)
            t_ev[par] = evc
            mk.w(P, evc)
            s_ev[par] = P.emit(lambda e, tout=tout, hc=hc, dg=dg: e.dma_start(
                out=D_["w2t"].ap()[hc * 128:(hc + 1) * 128, dg * 512:(dg + 1) * 512],
                in_=tout), mk.dq(f"w2f_s{par}"))
            it += 1
    last_s1 = emax(s_ev[0]) + emax(s_ev[1])
    w2t_v = D_["w2t"].ap().rearrange("(h1 h2) d -> h1 h2 d", h2=128)
    w2f_v = D_["w2f"].ap().rearrange("(h1 h2) d -> h1 h2 d", h2=128)
    t_ev = [None, None]; s_ev = [None, None]; mm_ev = [None, None]
    it = 0
    for h2b in range(0, 128, 4):
        for dg in range(4):
            par = it % 2
            tin = CT[0:64, par * 2048:(par + 1) * 2048]
            tout = CT[0:64, 4096 + par * 2048:4096 + (par + 1) * 2048]
            mk.w(S, last_s1, mm_ev[par], gate_ev)
            ld = S.emit(lambda e, tin=tin, h2b=h2b, dg=dg: e.dma_start(
                out=tin.rearrange("p (h2 d) -> p h2 d", d=512),
                in_=w2t_v[:, h2b:h2b + 4, dg * 512:(dg + 1) * 512]), mk.dq(f"w2f2_l{par}"))
            for j in range(4):
                mk.w(T, ld, ev["h64_b"], t_ev[par])
                evm = T.emit(lambda e, tin=tin, j=j: e.matmul(
                    D_["pb"][2 + j][0:64, :], D_["h64_b"][:],
                    tin[:, j * 512:(j + 1) * 512], start=True, stop=True), cT)
                if j == 3:
                    mm_ev[par] = evm
                mk.w(V, evm, s_ev[par])
                evc = V.emit(lambda e, tout=tout, j=j: e.tensor_copy(
                    tout[:, j * 512:(j + 1) * 512], D_["pb"][2 + j][0:64, :]), cV)
                t_ev[par] = evc
            mk.w(P, t_ev[par])
            s_ev[par] = P.emit(lambda e, tout=tout, h2b=h2b, dg=dg: e.dma_start(
                out=w2f_v[:, h2b:h2b + 4, dg * 512:(dg + 1) * 512],
                in_=tout.rearrange("p (h2 d) -> p h2 d", d=512)), mk.dq(f"w2f2_s{par}"))
            it += 1
    return emax(s_ev[0], s_ev[1])


GRPS = [(0, 512), (512, 512), (1024, 512), (1536, 512), (2048, 128)]


def fence(mk):
    cs = [mk.cT, mk.cV, mk.cS, mk.cP, mk.dD, mk.dS, mk.dP] + [
        c for n, c in mk._dmas.items() if not n.startswith("cast")]
    return [(c, c.n) for c in cs if c.n > 0]


def fence_all(mk):
    F = fence(mk)
    for st_ in (mk.T, mk.V, mk.S, mk.P, mk.Dq):
        mk.w(st_, F)
    return F


def phase1(mk, D_, ev):
    g = lambda k: D_[k]
    T, V, S, P, Dq = mk.T, mk.V, mk.S, mk.P, mk.Dq
    cT, cV, cS, cP, dD, dS, dP = mk.cT, mk.cV, mk.cS, mk.cP, mk.dD, mk.dS, mk.dP
    XF, CT, WS, QP, AXF = g("XF"), g("CT"), g("WS"), g("QP"), g("AXF")
    smallf, smalli = g("smallf"), g("smalli")
    pb = g("pb")

    for dc in range(16):
        mk.w(Dq, ev[f"xbf{dc // 4}"])
        e1 = Dq.emit(lambda e, dc=dc: e.dma_start(
            out=XF[:, dc * TPAD:(dc + 1) * TPAD],
            in_=D_["xbf"].ap()[:, dc * 128:(dc + 1) * 128], transpose=True),
            mk.dq("xfm"))
    ev["xfm"] = e1

    ev["magic"] = V.emit(lambda e: e.memset(smalli[:, 16:32], 0x5F3759DF), cV)
    D_["__magic_ev"] = ev["magic"]

    RAWW = {"wvb": "wv", "wgb": "wg", "wkb": "wk", "wqb": "wq"}

    def tm_pass(wname, dst, silu, with_beta):
        wsrc = D_[RAWW[wname]]
        wbuf = [WS[:, 0:8192], CT[:, 0:8192]]
        ebuf = [CT[:, 8448 + i * 512:8448 + (i + 1) * 512] for i in range(2)]
        wld = [None, None]; wuse = [None, None]
        bank_free = {0: None, 1: None, 2: None}
        ebuf_free = [None, None]
        bet_ev = None
        last_store = None
        evbw = None
        if with_beta:
            evbw = P.emit(lambda e: e.dma_start(
                out=CT[:, 8192:8448].rearrange("p (dc n) -> p dc n", n=16),
                in_=D_["wb"].rearrange("(dc p) n -> p dc n", p=128)), mk.dq("wbb1"))
        it = 0
        for ob in range(4):
            par = ob % 2
            mk.w(P, wuse[par])
            wld[par] = P.emit(lambda e, par=par, ob=ob: e.dma_start(
                out=wbuf[par].rearrange("p (dc n) -> p dc n", n=512),
                in_=wsrc.rearrange("(dc p) n -> p dc n", p=128)[:, :, ob * 512:(ob + 1) * 512]),
                mk.dq(f"tmw{par}"))
            for tc in range(16):
                bank = it % 2
                mk.w(T, ev["xfm"], wld[par], bank_free[bank])
                for dc in range(16):
                    evm = T.emit(lambda e, dc=dc, tc=tc, par=par, bank=bank: e.matmul(
                        pb[bank][:, :],
                        XF[:, dc * TPAD + HALO + tc * 128:dc * TPAD + HALO + (tc + 1) * 128],
                        wbuf[par][:, dc * 512:(dc + 1) * 512],
                        start=(dc == 0), stop=(dc == 15)), cT if dc == 15 else None)
                wuse[par] = evm
                ep = it % 2
                if silu:
                    mk.w(S, evm, ebuf_free[ep])
                    evc = S.emit(lambda e, bank=bank, ep=ep: e.activation(
                        ebuf[ep], pb[bank][:, :], AF.Silu), cS)
                else:
                    mk.w(V, evm, ebuf_free[ep])
                    evc = V.emit(lambda e, bank=bank, ep=ep: e.tensor_copy(
                        ebuf[ep], pb[bank][:, :]), cV)
                bank_free[bank] = evc
                mk.w(P, evc)
                est = P.emit(lambda e, ep=ep, tc=tc, ob=ob: e.dma_start(
                    out=D_[dst].ap()[tc * 128:(tc + 1) * 128, ob * 512:(ob + 1) * 512],
                    in_=ebuf[ep]), mk.dq(f"tmst{ep}"))
                ebuf_free[ep] = est
                last_store = est
                it += 1
        if with_beta:
            for tc in range(16):
                mk.w(T, evbw, bank_free[2])
                for dc in range(16):
                    evm = T.emit(lambda e, dc=dc, tc=tc: e.matmul(
                        pb[2][:, 0:16],
                        XF[:, dc * TPAD + HALO + tc * 128:dc * TPAD + HALO + (tc + 1) * 128],
                        CT[:, 8192 + dc * 16:8192 + (dc + 1) * 16],
                        start=(dc == 0), stop=(dc == 15)), cT if dc == 15 else None)
                mk.w(V, evm)
                evc = V.emit(lambda e, tc=tc: e.tensor_copy(
                    D_["beta_cm"][:, tc * 16:(tc + 1) * 16], pb[2][:, 0:16]), cV)
                bank_free[2] = evc
                bet_ev = evc
        return last_store, bet_ev

    ev["v_s"], ev["betaz"] = tm_pass("wvb", "v_s", silu=False, with_beta=True)
    fence_all(mk)
    ev["w2b"] = P.emit(lambda e: e.dma_start(out=D_["w2b"].ap()[:, :], in_=g("w2")),
                       mk.dq("cast1"))

    bcm, lcm, bclamp = D_["beta_cm"], D_["l_cm"], D_["lnb_cm"]
    dec_ncl, dec_g = D_["dec_ncl"], D_["dec_g"]
    dec_phi, dec_p, dec_cum, dec_slr = D_["dec_phi"], D_["dec_p"], D_["dec_cum"], D_["dec_slr"]
    e_pv = None
    for tc in range(16):
        mk.w(V, ev["betaz"], ev["bbr"], e_pv)
        e_pv = V.emit(lambda e, tc=tc: e.tensor_tensor(
            bcm[:, tc * 16:(tc + 1) * 16], bcm[:, tc * 16:(tc + 1) * 16],
            D_["bbr"][:, :], ALU.add), cV)
    mk.w(S, e_pv)
    e_sig = S.emit(lambda e: e.activation(bcm[:, :], bcm[:, :], AF.Sigmoid), cS)
    mk.w(V, e_sig)
    e_a1 = V.emit(lambda e: e.tensor_scalar(lcm[:, :], bcm[:, :], -1.0, 1.0,
                                            ALU.mult, ALU.add), cV)
    mk.w(V, e_a1)
    e_a2 = V.emit(lambda e: e.tensor_scalar_max(lcm[:, :], lcm[:, :], CLAMP), cV)
    e_b1 = V.emit(lambda e: e.tensor_scalar_max(bclamp[:, :], bcm[:, :], CLAMP), cV)
    mk.w(S, e_a2, e_b1)
    e_l = S.emit(lambda e: e.activation(lcm[:, :], lcm[:, :], AF.Ln), cS)
    e_lnb = S.emit(lambda e: e.activation(bclamp[:, :], bclamp[:, :], AF.Ln), cS)
    ecv = None
    for tc in range(16):
        mk.w(T, e_l, ev["ul_f"], ev["ones_f"], ecv)
        T.emit(lambda e, tc=tc: e.matmul(
            pb[3][:, 0:16], D_["ul_f"][:], lcm[:, tc * 16:(tc + 1) * 16],
            start=True, stop=True), None)
        em2 = T.emit(lambda e, tc=tc: e.matmul(
            pb[3][:, 16:32], D_["ones_f"][:], lcm[:, tc * 16:(tc + 1) * 16],
            start=True, stop=True), cT)
        mk.w(V, em2)
        V.emit(lambda e, tc=tc: e.tensor_copy(
            dec_cum[:, tc * 16:(tc + 1) * 16], pb[3][:, 0:16]), None)
        ecv = V.emit(lambda e, tc=tc: e.tensor_copy(
            dec_slr[:, tc * 16:(tc + 1) * 16], pb[3][:, 16:32]), cV)
    mk.w(V, e_lnb, ecv)
    e_ncl = V.emit(lambda e: e.tensor_sub(dec_ncl[:, :], bclamp[:, :], dec_cum[:, :]), cV)
    mk.w(V, e_ncl)
    e_garg = V.emit(lambda e: e.tensor_add(dec_g[:, :], dec_slr[:, :], dec_ncl[:, :]), cV)
    mk.w(S, e_garg, ecv)
    e_g = S.emit(lambda e: e.activation(dec_g[:, :], dec_g[:, :], AF.Exp), cS)
    e_phi = S.emit(lambda e: e.activation(dec_phi[:, :], dec_slr[:, :], AF.Exp), cS)
    e_p = S.emit(lambda e: e.activation(dec_p[:, :], dec_cum[:, :], AF.Exp), cS)
    mk.w(V, ecv)
    e_pt = V.emit(lambda e: e.reduce_sum(
        D_["phitot"][:, :],
        dec_slr[:, :].rearrange("p (c h) -> p h c", h=16), axis=AX.X), cV)
    mk.w(S, e_pt)
    e_pt2 = S.emit(lambda e: e.activation(D_["phitot"][:, :], D_["phitot"][:, :], AF.Exp), cS)
    ev["dec_g"] = e_g; ev["dec_phi"] = e_phi; ev["dec_p"] = e_p
    ev["dec_ncl"] = e_ncl; ev["dec_l"] = e_l; ev["phitot"] = e_pt2
    fence_all(mk)

    def fm_pass(wname, cw, cb, gamsrc, betsrc, rowbase, predst, lndst, do_u):
        wsrc = D_[RAWW[wname]]
        wbuf = [WS[:, 0:2048], WS[:, 2048:4096]]
        qsl = [QP[:, 0:TPAD], QP[:, TPAD:2 * TPAD]]
        acc = [QP[:, 2 * TPAD:2 * TPAD + 2048], QP[:, 2 * TPAD + 2048:2 * TPAD + 4096]]
        wld = [None, None]; wuse = [None, None]
        bank_free = {0: None, 1: None}
        conv_done = [None, None]
        store_ev = [None, None]
        gam, bet = D_["gam"], D_["bet"]
        evl = S.emit(lambda e: e.dma_start(
            out=D_["rowf"][0:1, 0:128], in_=D_[gamsrc]), mk.dq("rowg1"))
        evl2 = S.emit(lambda e: e.dma_start(
            out=D_["rowf"][0:1, 128:256], in_=D_[betsrc]), mk.dq("rowg2"))
        mk.w(T, evl, evl2, ev["ones_f"])
        eg1 = T.emit(lambda e: e.matmul(
            pb[5][:, 0:128], D_["ones_f"][0:1, :],
            D_["rowf"][0:1, 0:128], start=True, stop=True), cT)
        eg2 = T.emit(lambda e: e.matmul(
            pb[5][:, 128:256], D_["ones_f"][0:1, :],
            D_["rowf"][0:1, 128:256], start=True, stop=True), cT)
        mk.w(V, eg2)
        ec1 = V.emit(lambda e: e.tensor_copy(gam[:, :], pb[5][:, 0:128]), cV)
        e_gb = V.emit(lambda e: e.tensor_copy(bet[:, :], pb[5][:, 128:256]), cV)

        CE, cC = V, cV

        for s in range(16):
            par = s % 2
            mk.w(P, wuse[par])
            wld[par] = P.emit(lambda e, par=par, s=s: e.dma_start(
                out=wbuf[par].rearrange("p (dc n) -> p dc n", n=128),
                in_=wsrc.rearrange("(dc p) n -> p dc n", p=128)[:, :, s * 128:(s + 1) * 128]),
                mk.dq(f"fmw{par}"))
            for gi, (g0, gw) in enumerate(GRPS):
                bank = (s * len(GRPS) + gi) % 2
                mk.w(T, ev["xfm"], wld[par], bank_free[bank])
                for dc in range(16):
                    evm = T.emit(lambda e, dc=dc, g0=g0, gw=gw, par=par, bank=bank: e.matmul(
                        pb[bank][:, 0:gw],
                        wbuf[par][:, dc * 128:(dc + 1) * 128],
                        XF[:, dc * TPAD + g0:dc * TPAD + g0 + gw],
                        start=(dc == 0), stop=(dc == 15)), cT if dc == 15 else None)
                wuse[par] = evm
                mk.w(V, evm, conv_done[par], store_ev[par])
                bank_free[bank] = V.emit(lambda e, bank=bank, par=par, g0=g0, gw=gw: e.tensor_copy(
                    qsl[par][:, g0:g0 + gw], pb[bank][:, 0:gw]), cV)
            mk.w(CE, bank_free[0], bank_free[1], ev[cw], store_ev[par])
            prev = CE.emit(lambda e, par=par, s=s: e.tensor_scalar(
                acc[par][:, :], qsl[par][:, HALO - 3:HALO - 3 + 2048],
                D_[cw][:, s * 4:s * 4 + 1], None, ALU.mult), cC)
            for j in range(1, 4):
                mk.w(CE, prev)
                prev = CE.emit(lambda e, par=par, s=s, j=j: e.scalar_tensor_tensor(
                    acc[par][:, :], qsl[par][:, HALO - 3 + j:HALO - 3 + j + 2048],
                    D_[cw][:, s * 4 + j:s * 4 + j + 1], acc[par][:, :],
                    ALU.mult, ALU.add), cC)
            conv_done[par] = prev
            mk.w(S, prev, ev[cb])
            e_silu = S.emit(lambda e, par=par, s=s: e.activation(
                qsl[par][:, 0:2048], acc[par][:, :], AF.Silu,
                bias=D_[cb][:, s:s + 1]), cS)
            STq, dSTq = S, dS
            mk.w(STq, e_silu)
            store_ev[par] = STq.emit(lambda e, par=par, s=s: e.dma_start(
                out=D_[predst].ap()[s * 128:(s + 1) * 128, :],
                in_=qsl[par][:, 0:2048]), mk.dq(f"fmst{par}"))
        all_stores = store_ev[1]

        tmin = [CT[:, 0 + i * 2048:0 + (i + 1) * 2048] for i in range(2)]
        tmo1 = [CT[:, 4096 + i * 2048:4096 + (i + 1) * 2048] for i in range(2)]
        tmo2 = [CT[:, 8192 + i * 2048:8192 + (i + 1) * 2048] for i in range(2)]
        vch = [CT[:, 12288 + i * 2048:12288 + (i + 1) * 2048] for i in range(2)]
        ktg = D_["ktgb"]
        tmin_use = [None, None]
        tmo1_use = [None, None]
        tmo2_use = [None, None]
        vld = [None, None]; vuse = [None, None]
        e_sq_use = None
        uev = {h: None for h in range(16)}
        prev_kt = None
        prev_um = None
        prev_stt = None
        if do_u:
            e_u0 = V.emit(lambda e: e.memset(D_["uhat"][:, :], 0.0), cV)
            for h in range(16):
                uev[h] = e_u0
        last_st = None
        for tc in range(16):
            par = tc % 2
            mk.w(Dq, all_stores, store_ev[0], tmin_use[par])
            ld = Dq.emit(lambda e, tc=tc, par=par: e.dma_start(
                out=tmin[par], in_=D_[predst].ap()[:, tc * 128:(tc + 1) * 128],
                transpose=True), mk.dq(f"tmin{par}"))
            if do_u:
                mk.w(S, ev["v_s"], vuse[par])
                vld[par] = S.emit(lambda e, tc=tc, par=par: e.dma_start(
                    out=vch[par], in_=D_["v_s"].ap()[tc * 128:(tc + 1) * 128, :]),
                    mk.dq(f"vch{par}"))
            mk.w(S, ld, e_sq_use)
            e_sq = S.emit(lambda e, par=par: e.activation(
                AXF[:, 0:2048], tmin[par], AF.Square), cS)
            mk.w(V, ld)
            e_r1 = V.emit(lambda e, par=par: e.reduce_sum(
                smallf[:, 0:16], tmin[par].rearrange("p (h d) -> p h d", d=128),
                axis=AX.X), cV)
            mk.w(V, e_sq)
            e_r2 = V.emit(lambda e: e.reduce_sum(
                smallf[:, 16:32], AXF[:, 0:2048].rearrange("p (h d) -> p h d", d=128),
                axis=AX.X), cV)
            e_sq_use = e_r2
            mk.w(V, e_r1)
            e_m = V.emit(lambda e: e.tensor_scalar_mul(
                smallf[:, 32:48], smallf[:, 0:16], 1.0 / 128), cV)
            mk.w(V, e_r2)
            e_x2 = V.emit(lambda e: e.tensor_scalar_mul(
                smallf[:, 48:64], smallf[:, 16:32], 1.0 / 128), cV)
            mk.w(V, e_m)
            e_m2 = V.emit(lambda e: e.tensor_mul(
                smallf[:, 64:80], smallf[:, 32:48], smallf[:, 32:48]), cV)
            mk.w(V, e_m2, e_x2)
            e_var = V.emit(lambda e: e.tensor_sub(
                smallf[:, 80:96], smallf[:, 48:64], smallf[:, 64:80]), cV)
            mk.w(V, e_var)
            e_vr = V.emit(lambda e: e.tensor_scalar_add(
                smallf[:, 80:96], smallf[:, 80:96], LNEPS), cV)
            mk.w(V, e_vr, ev["magic"])
            e_i1 = V.emit(lambda e: e.tensor_scalar(
                smalli[:, 0:16], smallf[:, 80:96].bitcast(I32), 1, None,
                ALU.arith_shift_right), cV)
            mk.w(V, e_i1)
            e_prev = V.emit(lambda e: e.tensor_sub(
                smalli[:, 32:48], smalli[:, 16:32], smalli[:, 0:16]), cV)
            ycur = smalli[:, 32:48].bitcast(F32)
            for itn in range(2):
                mk.w(V, e_prev)
                e_y2 = V.emit(lambda e, ycur=ycur: e.tensor_mul(
                    smallf[:, 96:112], ycur, ycur), cV)
                mk.w(V, e_y2)
                e_t = V.emit(lambda e: e.tensor_mul(
                    smallf[:, 96:112], smallf[:, 96:112], smallf[:, 80:96]), cV)
                mk.w(V, e_t)
                e_t2 = V.emit(lambda e: e.tensor_scalar(
                    smallf[:, 96:112], smallf[:, 96:112], -0.5, 1.5,
                    ALU.mult, ALU.add), cV)
                mk.w(V, e_t2)
                e_prev = V.emit(lambda e, ycur=ycur: e.tensor_mul(
                    smallf[:, 112:128], ycur, smallf[:, 96:112]), cV)
                ycur = smallf[:, 112:128]
            e_rstd = e_prev
            mk.w(V, e_rstd, tmo1_use[par])
            ap_ev = None
            for h in range(16):
                ap_ev = V.emit(lambda e, par=par, h=h: e.tensor_scalar(
                    tmo1[par][:, h * 128:(h + 1) * 128], tmin[par][:, h * 128:(h + 1) * 128],
                    smallf[:, 32 + h:33 + h], smallf[:, 112 + h:113 + h],
                    ALU.subtract, ALU.mult), cV)
            tmin_use[par] = [ap_ev, e_sq]
            mk.w(P, ap_ev, e_gb, tmo2_use[par])
            e_gm = None
            for h in range(16):
                e_gm = P.emit(lambda e, par=par, h=h: e.tensor_mul(
                    tmo2[par][:, h * 128:(h + 1) * 128],
                    tmo1[par][:, h * 128:(h + 1) * 128], gam[:, :]), cP)
            mk.w(P, e_gm)
            e_bt = None
            for h in range(16):
                e_bt = P.emit(lambda e, par=par, h=h: e.tensor_add(
                    tmo2[par][:, h * 128:(h + 1) * 128],
                    tmo2[par][:, h * 128:(h + 1) * 128], bet[:, :]), cP)
            tmo1_use[par] = e_bt
            mk.w(S, e_bt)
            est = S.emit(lambda e, tc=tc, par=par: e.dma_start(
                out=D_[lndst].ap()[tc * 128:(tc + 1) * 128, :], in_=tmo2[par]),
                mk.dq(f"lns{par}"))
            tmo2_use[par] = [est]
            last_st = est
            if do_u:
                for h in range(16):
                    mk.w(V, e_bt, ev["dec_g"], prev_um)
                    e_kt = V.emit(lambda e, par=par, h=h, tc=tc: e.tensor_scalar(
                        ktg[:, 0:128], tmo2[par][:, h * 128:(h + 1) * 128],
                        D_["dec_g"][:, tc * 16 + h:tc * 16 + h + 1], None, ALU.mult), cV)
                    prev_kt = e_kt
                    mk.w(T, e_kt, vld[par], prev_stt)
                    e_um = T.emit(lambda e, par=par, h=h: e.matmul(
                        pb[4][:, 0:128], ktg[:, 0:128], vch[par][:, h * 128:(h + 1) * 128],
                        start=True, stop=True), cT)
                    prev_um = e_um
                    vuse[par] = e_um
                    mk.w(V, e_um, ev["dec_phi"], uev[h])
                    uev[h] = prev_stt = V.emit(lambda e, h=h, tc=tc: e.scalar_tensor_tensor(
                        D_["uhat"][:, h * 128:(h + 1) * 128],
                        D_["uhat"][:, h * 128:(h + 1) * 128],
                        D_["dec_phi"][:, tc * 16 + h:tc * 16 + h + 1],
                        pb[4][:, 0:128], ALU.mult, ALU.add), cV)
                tmo2_use[par].append(prev_kt)
        return last_st, (uev if do_u else None)

    ev["kn_s"], uev = fm_pass("wkb", "cwk", "cbk", "knw", "knb", 2, "kpre", "kn_s", True)
    ev["uhat"] = emax(*uev.values())
    fence_all(mk)
    phase2a(mk, D_, ev)
    ev = dmat_pass(mk, D_, ev)
    mbk = [D_["CT"][:, 16384 + i * 2048:16384 + (i + 1) * 2048] for i in range(2)]
    ev["kn_f"] = mirror_pass(mk, D_, "kn_s", "kn_f", 16, mbk, ev["kn_s"])
    ev["w2f_done"] = w2f_fold(mk, D_, ev, None)
    fence_all(mk)
    ev["qn_s"], _ = fm_pass("wqb", "cwq", "cbq", "qnw", "qnb", 0, "qpre", "qn_s", False)
    fence_all(mk)
    ev["w1b"] = P.emit(lambda e: e.dma_start(out=D_["w1b"].ap()[:, :], in_=g("w1")),
                       mk.dq("cast2"))
    ev["w3b"] = P.emit(lambda e: e.dma_start(out=D_["w3b"].ap()[:, :], in_=g("w3")),
                       mk.dq("cast3"))
    ev["g_s"], _ = tm_pass("wgb", "g_s", silu=True, with_beta=False)
    ev = phase2(mk, D_, ev)
    ev["gate_end"] = (mk.cT, mk.cT.n)
    ev["gate_end_v"] = (mk.cV, mk.cV.n)
    ev["gate_end_s"] = (mk.cS, mk.cS.n)
    return ev


def phase2a(mk, D_, ev):
    S, P = mk.S, mk.P
    mk.w(S, ev["phitot"])
    e_s1 = S.emit(lambda e: e.dma_start(
        out=D_["csum_in"].ap()[0:1, 0:16], in_=D_["phitot"][0:1, 0:16]), mk.dq("cs1"))
    mk.w(S, ev["uhat"])
    e_s2 = S.emit(lambda e: e.dma_start(
        out=D_["csum_in"].ap()[1:17, :].rearrange("h (dk dv) -> dk h dv", dk=128, dv=128),
        in_=D_["uhat"][:, :].rearrange("p (h dv) -> p h dv", dv=128)), mk.dq("cs2"))
    mk.w(P, e_s1, e_s2)
    ev["cc"] = P.emit(lambda e: e.collective_compute(
        "AllGather", ALU.bypass, replica_groups=[list(range(8))],
        ins=[D_["csum_in"].ap().opt()], outs=[D_["csum_out"].ap().opt()]), mk.cc)
    return ev


def phase2(mk, D_, ev):
    g = lambda k: D_[k]
    T, V, S, P, Dq = mk.T, mk.V, mk.S, mk.P, mk.Dq
    cT, cV, cS, cP, dD, dP = mk.cT, mk.cV, mk.cS, mk.cP, mk.dD, mk.dP
    dS = mk.dS
    pb, AXF = g("pb"), g("AXF")
    e_cc = ev["cc"]

    mk.w(S, e_cc)
    e_l1 = S.emit(lambda e: e.dma_start(
        out=D_["rowf"][0:1, 280:408].rearrange("s (r n) -> s r n", n=16),
        in_=D_["csum_out"].ap().rearrange("(r s) n -> s r n", s=17)[0:1, :, 0:16]), mk.dq("cl1"))
    e_l2 = S.emit(lambda e: e.dma_start(out=D_["rowf"][0:1, 272:280], in_=D_["c_pm"]), mk.dq("cl2"))
    mk.w(T, e_l1, e_l2, ev["ones_f"])
    e_m1 = T.emit(lambda e: e.matmul(pb[5][:, 0:128], D_["ones_f"][0:1, :],
                                     D_["rowf"][0:1, 280:408], start=True, stop=True), cT)
    e_m2 = T.emit(lambda e: e.matmul(pb[5][:, 128:136], D_["ones_f"][0:1, :],
                                     D_["rowf"][0:1, 272:280], start=True, stop=True), cT)
    mk.w(V, e_m1, e_m2)
    e_pg = V.emit(lambda e: e.tensor_copy(D_["phig"][:, :], pb[5][:, 0:128]), cV)
    e_pm = V.emit(lambda e: e.tensor_copy(D_["pm_r"][:, 0:8], pb[5][:, 128:136]), cV)
    mk.w(V, e_pm)
    e_pm2 = V.emit(lambda e: e.tensor_scalar(D_["pm_r"][:, 8:16], D_["pm_r"][:, 0:8],
                                             -1.0, 1.0, ALU.mult, ALU.add), cV)
    wc = D_["wcomp"]
    e_f = V.emit(lambda e: e.memset(wc[:, 128:144], 1.0), cV)
    cur, nxt = 144, 128
    cur = 128
    nxt = 144
    for c in range(7, -1, -1):
        mk.w(V, e_f, e_pg, e_pm2)
        e_w = V.emit(lambda e, c=c, cur=cur: e.tensor_scalar(
            wc[:, c * 16:(c + 1) * 16], wc[:, cur:cur + 16],
            D_["pm_r"][:, c:c + 1], None, ALU.mult), cV)
        mk.w(V, e_w)
        e_t = V.emit(lambda e, c=c: e.tensor_scalar(
            D_["smallf"][:, 128:144], D_["phig"][:, c * 16:(c + 1) * 16],
            D_["pm_r"][:, c:c + 1], D_["pm_r"][:, 8 + c:9 + c],
            ALU.mult, ALU.add), cV)
        mk.w(V, e_t)
        e_f = V.emit(lambda e, cur=cur, nxt=nxt: e.tensor_mul(
            wc[:, nxt:nxt + 16], wc[:, cur:cur + 16], D_["smallf"][:, 128:144]), cV)
        cur, nxt = nxt, cur
    e_wdone = e_f

    e_sa0 = V.emit(lambda e: e.memset(D_["SS"][:, :], 0.0), cV)
    csv = D_["csum_out"].ap().rearrange("(r s) n -> r s n", s=17)
    prev_ld = None
    prev_stt = e_sa0
    for c in range(8):
        mk.w(S, e_cc, prev_stt)
        e_ld = S.emit(lambda e, c=c: e.dma_start(
            out=AXF[:, 0:2048].rearrange("p (h dv) -> p h dv", dv=128),
            in_=csv[c, 1:17, :].rearrange("h (dk dv) -> dk h dv", dk=128, dv=128)), mk.dq("cug"))
        for h in range(16):
            mk.w(V, e_ld, e_wdone, prev_stt)
            prev_stt = V.emit(lambda e, c=c, h=h: e.scalar_tensor_tensor(
                D_["SS"][:, h * 128:(h + 1) * 128], AXF[:, h * 128:(h + 1) * 128],
                wc[:, c * 16 + h:c * 16 + h + 1], D_["SS"][:, h * 128:(h + 1) * 128],
                ALU.mult, ALU.add), cV)
        prev_ld = e_ld
    mk.w(V, prev_stt)
    e_sbf = V.emit(lambda e: e.tensor_copy(D_["SBF"][:, :], D_["SS"][:, :]), cV)
    ev["SBF0"] = e_sbf
    return ev


def mirror_pass(mk, D_, src_name, dst_name, ncols, buf, gate, nrows=TPC, stq=None,
                dual=False):
    Dq, S = mk.Dq, (stq if stq is not None else mk.S)
    bufs = buf if isinstance(buf, list) else [buf]
    use = [None] * len(bufs)
    evs = []
    for qc in range(ncols):
        p = qc % len(bufs)
        trq = (mk.Dq if qc % 2 == 0 else mk.S) if dual else Dq
        sq = (mk.S if qc % 2 == 0 else mk.Dq) if dual else S
        mk.w(trq, gate, use[p])
        ld = trq.emit(lambda e, qc=qc, p=p: e.dma_start(
            out=bufs[p][:, 0:nrows],
            in_=D_[src_name].ap()[:, qc * 128:(qc + 1) * 128], transpose=True),
            mk.dq(f"mirl{p}"))
        mk.w(sq, ld)
        st = sq.emit(lambda e, qc=qc, p=p: e.dma_start(
            out=D_[dst_name].ap()[qc * 128:(qc + 1) * 128, :], in_=bufs[p][:, 0:nrows]),
            mk.dq(f"mirs{p}"))
        use[p] = st
        evs = use
    return [e for e in evs if e is not None]


def phase3(mk, D_, ev):
    g = lambda k: D_[k]
    T, V, S, P, Dq = mk.T, mk.V, mk.S, mk.P, mk.Dq
    cT, cV, cS, cP, dD, dS, dP = mk.cT, mk.cV, mk.cS, mk.cP, mk.dD, mk.dS, mk.dP
    XF, CT, AXF, pb = g("XF"), g("CT"), g("AXF"), g("pb")
    uld, drb, dmb, qkb, amb, ktg, oib = (g("uldb"), g("drawb"), g("dmatb"),
                                         g("qkmb"), g("amatb"), g("ktgb"), g("ointb"))
    mk.w(T, ev["gate_end_v"], ev["gate_end_s"])
    mb = [CT[:, 16384 + i * 2048:16384 + (i + 1) * 2048] for i in range(2)]
    ev["qn_f"] = mirror_pass(mk, D_, "qn_s", "qn_f", 16, mb, [ev["qn_s"], ev["kn_f"]])
    wb1 = XF[:, 24576:32768]
    ev["w1t_f"] = mirror_pass(mk, D_, "w1b", "w1t_f", 16, wb1,
                              [ev["w1b"], ev["qn_f"], ev["gate_end"]], nrows=FH,
                              stq=mk.P)
    ev["w3t_f"] = mirror_pass(mk, D_, "w3b", "w3t_f", 16, wb1,
                              [ev["w3b"], ev["w1t_f"]], nrows=FH, stq=mk.P)

    qf = [CT[:, 0:4096], CT[:, 4096:8192]]
    kf = [CT[:, 8192:12288], CT[:, 12288:16384]]
    gch = [XF[:, i * 2048:(i + 1) * 2048] for i in range(2)]
    ogb = [XF[:, 4096 + i * 2048:4096 + (i + 1) * 2048] for i in range(2)]
    obf = [XF[:, 8192 + i * 2048:8192 + (i + 1) * 2048] for i in range(2)]
    vch = [XF[:, 12288 + i * 2048:12288 + (i + 1) * 2048] for i in range(2)]
    ktm = [XF[:, 16384 + i * 2048:16384 + (i + 1) * 2048] for i in range(2)]
    amb4 = [XF[:, 20480 + i * 512:20480 + (i + 1) * 512] for i in range(2)]
    oib4 = [XF[:, 21504 + i * 512:21504 + (i + 1) * 512] for i in range(2)]
    ktg4 = [XF[:, 22528 + i * 512:22528 + (i + 1) * 512] for i in range(2)]
    qf_use = [ev["gate_end"], ev["gate_end"]]; kf_use = [ev["gate_end"], ev["gate_end"]]
    ktm_use = [ev["gate_end"], ev["gate_end"]]; v_use = [ev["gate_end"], ev["gate_end"]]
    g_use = [ev["gate_end"], ev["gate_end"]]; og_use = [None, None]
    ob_use = [ev["gate_end"], ev["gate_end"]]
    qf_ld = [None, None]; ktm_ld = [None, None]; v_ld = [None, None]; g_ld = [None, None]
    pb_use = {i: None for i in range(8)}
    buf_use = {("amt", 0): ev["gate_end"], ("amt", 1): ev["gate_end"],
               ("ktg", 0): ev["gate_end"], ("ktg", 1): ev["gate_end"],
               ("oin", 0): ev["gate_end"], ("oin", 1): ev["gate_end"]}
    sbf_ev = {h: ev["SBF0"] for h in range(16)}
    su_ev = {h: ev["SBF0"] for h in range(16)}
    dmch = mb
    dm_use = [ev["qn_f"], ev["qn_f"]]
    dm_ld = [None, None]
    e_og = None
    st_og = [None, None]
    ogtst = [XF[:, 23552 + i * 512:23552 + (i + 1) * 512] for i in range(2)]
    ogt_use = [None, None]
    for pc in range(8):
        par = pc % 2
        mk.w(S, ev["qn_f"], qf_use[par])
        qf_ld[par] = S.emit(lambda e, pc=pc, par=par: e.dma_start(
            out=qf[par].rearrange("p (h n) -> p h n", n=256),
            in_=D_["qn_f"].ap().rearrange("(h p) t -> p h t", p=128)[
                :, :, pc * 256:(pc + 1) * 256]), mk.dq(f"qf{par}"))
        mk.w(S, ev["kn_f"], kf_use[par])
        kf_ld = S.emit(lambda e, pc=pc, par=par: e.dma_start(
            out=kf[par].rearrange("p (h n) -> p h n", n=256),
            in_=D_["kn_f"].ap().rearrange("(h p) t -> p h t", p=128)[
                :, :, pc * 256:(pc + 1) * 256]), mk.dq(f"kf{par}"))
        for half in range(2):
            tc = pc * 2 + half
            tpar = tc % 2
            toff = half * 128
            mk.w(S, ev["kn_s"], ktm_use[tpar])
            ktm_ld[tpar] = S.emit(lambda e, tc=tc, tpar=tpar: e.dma_start(
                out=ktm[tpar], in_=D_["kn_s"].ap()[tc * 128:(tc + 1) * 128, :]),
                mk.dq(f"ktm{tpar}"))
            mk.w(S, ev["v_s"], v_use[tpar])
            v_ld[tpar] = S.emit(lambda e, tc=tc, tpar=tpar: e.dma_start(
                out=vch[tpar], in_=D_["v_s"].ap()[tc * 128:(tc + 1) * 128, :]),
                mk.dq(f"vc3{tpar}"))
            mk.w(S, ev["g_s"], g_use[tpar])
            g_ld[tpar] = S.emit(lambda e, tc=tc, tpar=tpar: e.dma_start(
                out=gch[tpar], in_=D_["g_s"].ap()[tc * 128:(tc + 1) * 128, :]),
                mk.dq(f"gc3{tpar}"))
            mk.w(S, ev["dmat"], dm_use[tpar])
            dm_ld[tpar] = S.emit(lambda e, tc=tc, tpar=tpar: e.dma_start(
                out=dmch[tpar], in_=D_["dmat"].ap()[tc * 128:(tc + 1) * 128, :]),
                mk.dq(f"dm3{tpar}"))
            e_oc = None
            prev = None
            last_su = None
            for g in range(5):
                if g < 4:
                    h0 = g * 4
                    gp = g % 2
                    mmQK = None
                    for j in range(4):
                        mk.w(T, qf_ld[par], kf_ld,
                             pb_use[gp] if j == 0 else None)
                        mmQK = T.emit(lambda e, h=h0 + j, j=j, par=par, toff=toff,
                                      gp=gp: e.matmul(
                            pb[gp][:, j * 128:(j + 1) * 128],
                            kf[par][:, h * 256 + toff:h * 256 + toff + 128],
                            qf[par][:, h * 256 + toff:h * 256 + toff + 128],
                            start=True, stop=True), cT)
                    mmI = None
                    for j in range(4):
                        mk.w(T, sbf_ev[h0] if j == 0 else None,
                             pb_use[4 + gp] if j == 0 else None)
                        mmI = T.emit(lambda e, h=h0 + j, j=j, par=par, toff=toff,
                                     gp=gp: e.matmul(
                            pb[4 + gp][:, j * 128:(j + 1) * 128],
                            qf[par][:, h * 256 + toff:h * 256 + toff + 128],
                            D_["SBF"][:, h * 128:(h + 1) * 128],
                            start=True, stop=True), cT)
                    mk.w(V, mmQK, dm_ld[tpar], buf_use.get(("amt", gp)))
                    e_am = V.emit(lambda e, h0=h0, gp=gp, tpar=tpar: e.tensor_tensor(
                        amb4[gp][:, 0:512], pb[gp][:, 0:512],
                        dmch[tpar][:, h0 * 128:(h0 + 4) * 128], ALU.mult), cV)
                    pb_use[gp] = e_am
                    dm_use[tpar] = e_am
                    e_oi = None
                    for j in range(4):
                        mk.w(S, mmI if j == 0 else None,
                             ev["dec_p"] if j == 0 else None,
                             buf_use.get(("oin", gp)) if j == 0 else None)
                        e_oi = S.emit(lambda e, h=h0 + j, j=j, tc=tc, gp=gp: e.activation(
                            oib4[gp][:, j * 128:(j + 1) * 128],
                            pb[4 + gp][:, j * 128:(j + 1) * 128], AF.Identity,
                            scale=D_["dec_p"][:, tc * 16 + h:tc * 16 + h + 1]), cS)
                    pb_use[4 + gp] = e_oi
                    e_kt = None
                    for j in range(4):
                        mk.w(V, ktm_ld[tpar] if j == 0 else None,
                             ev["dec_g"] if j == 0 else None,
                             buf_use.get(("ktg", gp)) if j == 0 else None)
                        e_kt = V.emit(lambda e, h=h0 + j, j=j, tc=tc, tpar=tpar,
                                      gp=gp: e.tensor_scalar(
                            ktg4[gp][:, j * 128:(j + 1) * 128],
                            ktm[tpar][:, h * 128:(h + 1) * 128],
                            D_["dec_g"][:, tc * 16 + h:tc * 16 + h + 1],
                            None, ALU.mult), cV)
                    cur = (h0, gp, e_am, e_oi, e_kt)
                else:
                    cur = None
                if prev is not None:
                    ph0, pgp, p_am, p_oi, p_kt = prev
                    mmO = None
                    for j in range(4):
                        mk.w(T, p_am if j == 0 else None, v_ld[tpar],
                             pb_use[2 + pgp] if j == 0 else None)
                        mmO = T.emit(lambda e, h=ph0 + j, j=j, tpar=tpar,
                                     gp=pgp: e.matmul(
                            pb[2 + gp][:, j * 128:(j + 1) * 128],
                            amb4[gp][:, j * 128:(j + 1) * 128],
                            vch[tpar][:, h * 128:(h + 1) * 128],
                            start=True, stop=True), cT)
                    buf_use[("amt", pgp)] = mmO
                    mmU = None
                    for j in range(4):
                        mk.w(T, p_kt if j == 0 else None,
                             pb_use[6 + pgp] if j == 0 else None)
                        mmU = T.emit(lambda e, h=ph0 + j, j=j, tpar=tpar,
                                     gp=pgp: e.matmul(
                            pb[6 + gp][:, j * 128:(j + 1) * 128],
                            ktg4[gp][:, j * 128:(j + 1) * 128],
                            vch[tpar][:, h * 128:(h + 1) * 128],
                            start=True, stop=True), cT)
                    buf_use[("ktg", pgp)] = mmU
                    mk.w(V, p_oi, mmO, ob_use[tpar])
                    e_oc = V.emit(lambda e, h0=ph0, gp=pgp, tpar=tpar: e.tensor_add(
                        obf[tpar][:, h0 * 128:(h0 + 4) * 128],
                        oib4[gp][:, 0:512], pb[2 + gp][:, 0:512]), cV)
                    pb_use[2 + pgp] = e_oc
                    buf_use[("oin", pgp)] = e_oc
                    mk.w(V, mmU, ev["dec_phi"], su_ev[ph0])
                    for j in range(4):
                        last_su = e_su = V.emit(lambda e, h=ph0 + j, tc=tc,
                                                gp=pgp: e.scalar_tensor_tensor(
                            D_["SS"][:, h * 128:(h + 1) * 128],
                            D_["SS"][:, h * 128:(h + 1) * 128],
                            D_["dec_phi"][:, tc * 16 + h:tc * 16 + h + 1],
                            pb[6 + gp][:, (h % 4) * 128:((h % 4) + 1) * 128],
                            ALU.mult, ALU.add), cV)
                    pb_use[6 + pgp] = last_su
                prev = cur
            mk.w(V, last_su)
            e_sbc = V.emit(lambda e: e.tensor_copy(D_["SBF"][:, :], D_["SS"][:, :]), cV)
            for h in range(16):
                sbf_ev[h] = e_sbc
                su_ev[h] = e_sbc
            qf_use[par] = mmI
            kf_use[par] = mmQK
            ktm_use[tpar] = e_kt
            v_use[tpar] = emax(mmU, mmO)
            mk.w(V, e_oc, g_ld[tpar], og_use[tpar])
            e_og = V.emit(lambda e, tpar=tpar: e.tensor_mul(
                ogb[tpar], obf[tpar], gch[tpar]), cV)
            ob_use[tpar] = e_og
            g_use[tpar] = e_og
            for jb in range(4):
                tb = jb % 2
                mmT = None
                for k in range(4):
                    j = jb * 4 + k
                    mk.w(T, e_og if k == 0 else None,
                         ev["eye_b"] if k == 0 else None,
                         pb_use[tb] if k == 0 else None)
                    mmT = T.emit(lambda e, j=j, k=k, tb=tb, tpar=tpar: e.matmul(
                        pb[tb][:, k * 128:(k + 1) * 128],
                        ogb[tpar][:, j * 128:(j + 1) * 128], D_["eye_b"][:],
                        start=True, stop=True), cT)
                mk.w(V, mmT, ogt_use[tb])
                e_tv = V.emit(lambda e, tb=tb: e.tensor_copy(
                    ogtst[tb], pb[tb][:, 0:512]), cV)
                pb_use[tb] = e_tv
                for k in range(4):
                    j = jb * 4 + k
                    mk.w(S, e_tv if k == 0 else None)
                    st_og[tpar] = S.emit(lambda e, j=j, k=k, tb=tb, tc=tc: e.dma_start(
                        out=D_["og_f"].ap()[j * 128:(j + 1) * 128,
                                            tc * 128:(tc + 1) * 128],
                        in_=ogtst[tb][:, k * 128:(k + 1) * 128]), mk.dq(f"ogt{tb}"))
                ogt_use[tb] = st_og[tpar]
            og_use[tpar] = mmT
    ev["og_f"] = emax(st_og[0], st_og[1])
    ev["og_s"] = ev["og_f"]
    fence_all(mk)

    e_wo = P.emit(lambda e: e.dma_start(
        out=XF[:, 0:32768].rearrange("p (vc d) -> p vc d", d=2048),
        in_=D_["wo"].rearrange("(vc p) d -> p vc d", p=128)), mk.dq("wo"))
    ogf = [CT[:, 0:8192], CT[:, 8192:16384]]
    sqb = XF[:, 32768:34816]
    QP = D_["QP"]
    x2b = [QP[:, i * 2048:(i + 1) * 2048] for i in range(2)]
    ogf_use = [None, None]
    x2_use = [None, None]
    x1_use = None
    sq_use = None
    sum_use = None
    pbu = {i: None for i in range(4)}
    last_x2 = None
    for grp in range(4):
        gp = grp % 2
        mk.w(S, ev["og_f"], ogf_use[gp])
        og_ld = S.emit(lambda e, grp=grp, gp=gp: e.dma_start(
            out=ogf[gp].rearrange("p (vc n) -> p vc n", n=512),
            in_=D_["og_f"].ap().rearrange("(vc p) t -> p vc t", p=128)[
                :, :, grp * 512:(grp + 1) * 512]), mk.dq(f"ogf{gp}"))
        for tcl in range(4):
            tok0 = grp * 512 + tcl * 128
            mk.w(S, x1_use)
            x1_ld = S.emit(lambda e, tok0=tok0: e.dma_start(
                out=AXF[:, 2048:4096],
                in_=D_["x_in"][HALO + tok0:HALO + tok0 + 128, :]), mk.dq("x1"))
            e_ep = None
            for ds in range(4):
                mk.w(T, og_ld, e_wo, pbu[ds])
                for vc in range(16):
                    mmW = T.emit(lambda e, vc=vc, ds=ds, tcl=tcl, gp=gp: e.matmul(
                        pb[ds][:, :], ogf[gp][:, vc * 512 + tcl * 128:vc * 512 + (tcl + 1) * 128],
                        XF[:, vc * 2048 + ds * 512:vc * 2048 + (ds + 1) * 512],
                        start=(vc == 0), stop=(vc == 15)), cT if vc == 15 else None)
                mk.w(V, mmW, x1_ld, sum_use)
                e_ep = V.emit(lambda e, ds=ds: e.tensor_add(
                    AXF[:, ds * 512:(ds + 1) * 512], pb[ds][:, :],
                    AXF[:, 2048 + ds * 512:2048 + (ds + 1) * 512]), cV)
                pbu[ds] = e_ep
            x1_use = e_ep
            ogf_use[gp] = mmW
            mk.w(S, e_ep, sq_use)
            e_sq = S.emit(lambda e: e.activation(sqb, AXF[:, 0:2048], AF.Square), cS)
            mk.w(V, e_ep)
            e_r1 = V.emit(lambda e: e.reduce_sum(
                D_["smallf"][:, 0:1], AXF[:, 0:2048].rearrange("p (a d) -> p a d", a=1),
                axis=AX.X), cV)
            mk.w(V, e_sq)
            e_r2 = V.emit(lambda e: e.reduce_sum(
                D_["smallf"][:, 1:2], sqb.rearrange("p (a d) -> p a d", a=1),
                axis=AX.X), cV)
            sq_use = e_r2
            e_rstd, e_m = ln_newton(mk, D_, e_r1, e_r2, 1.0 / D)
            mk.w(V, e_rstd, x2_use[tcl % 2])
            e_ap = V.emit(lambda e, tcl=tcl: e.tensor_scalar(
                x2b[tcl % 2], AXF[:, 0:2048], D_["smallf"][:, 32:33],
                D_["smallf"][:, 112:113], ALU.subtract, ALU.mult), cV)
            sum_use = e_ap
            mk.w(S, e_ap)
            e_st = S.emit(lambda e, tok0=tok0, tcl=tcl: e.dma_start(
                out=D_["x2_s"].ap()[tok0:tok0 + 128, :], in_=x2b[tcl % 2]),
                mk.dq(f"x2st{tcl % 2}"))
            x2_use[tcl % 2] = e_st
            last_x2 = e_st
    ev["x2_s"] = last_x2
    fence_all(mk)
    return ev


def ln_newton(mk, D_, e_sum, e_sumsq, inv_n):
    V, cV = mk.V, mk.cV
    smallf, smalli = D_["smallf"], D_["smalli"]
    mk.w(V, e_sum)
    e_m = V.emit(lambda e: e.tensor_scalar_mul(smallf[:, 32:33], smallf[:, 0:1], inv_n), cV)
    mk.w(V, e_sumsq)
    e_x2 = V.emit(lambda e: e.tensor_scalar_mul(smallf[:, 48:49], smallf[:, 1:2], inv_n), cV)
    mk.w(V, e_m)
    e_m2 = V.emit(lambda e: e.tensor_mul(smallf[:, 64:65], smallf[:, 32:33],
                                         smallf[:, 32:33]), cV)
    mk.w(V, e_m2, e_x2)
    e_var = V.emit(lambda e: e.tensor_sub(smallf[:, 80:81], smallf[:, 48:49],
                                          smallf[:, 64:65]), cV)
    mk.w(V, e_var)
    e_vr = V.emit(lambda e: e.tensor_scalar_add(smallf[:, 80:81], smallf[:, 80:81],
                                                LNEPS), cV)
    mk.w(V, e_vr, D_["__magic_ev"])
    e_i1 = V.emit(lambda e: e.tensor_scalar(smalli[:, 0:1], smallf[:, 80:81].bitcast(I32),
                                            1, None, ALU.arith_shift_right), cV)
    mk.w(V, e_i1)
    e_prev = V.emit(lambda e: e.tensor_sub(smalli[:, 32:33], smalli[:, 16:17],
                                           smalli[:, 0:1]), cV)
    ycur = smalli[:, 32:33].bitcast(F32)
    for itn in range(2):
        mk.w(V, e_prev)
        e_y2 = V.emit(lambda e, ycur=ycur: e.tensor_mul(smallf[:, 96:97], ycur, ycur), cV)
        mk.w(V, e_y2)
        e_t = V.emit(lambda e: e.tensor_mul(smallf[:, 96:97], smallf[:, 96:97],
                                            smallf[:, 80:81]), cV)
        mk.w(V, e_t)
        e_t2 = V.emit(lambda e: e.tensor_scalar(smallf[:, 96:97], smallf[:, 96:97],
                                                -0.5, 1.5, ALU.mult, ALU.add), cV)
        mk.w(V, e_t2)
        e_prev = V.emit(lambda e, ycur=ycur: e.tensor_mul(smallf[:, 112:113], ycur,
                                                          smallf[:, 96:97]), cV)
        ycur = smallf[:, 112:113]
    return e_prev, e_m


def phase4(mk, D_, ev, ev_w2f):
    g = lambda k: D_[k]
    T, V, S, P, Dq = mk.T, mk.V, mk.S, mk.P, mk.Dq
    cT, cV, cS, cP, dD, dS, dP = mk.cT, mk.cV, mk.cS, mk.cP, mk.dD, mk.dS, mk.dP
    XF, CT, WS, QP, AXF, pb = g("XF"), g("CT"), g("WS"), g("QP"), g("AXF"), g("pb")
    x2f = CT[:, 0:8192]
    sums = [CT[:, 8192 + i * 2048:8192 + (i + 1) * 2048] for i in range(4)]
    w2tile = [CT[:, 16384 + i * 1024:16384 + (i + 1) * 1024] for i in range(2)]
    x2tm = [CT[:, 18432 + i * 512:18432 + (i + 1) * 512] for i in range(2)]
    sil = [CT[:, 19456 + i * 512:19456 + (i + 1) * 512] for i in range(2)]
    sqb = XF[:, 32768:34816]
    outst = AXF[:, 0:2048]
    w1buf = [WS[:, 0:4096], WS[:, 4096:8192]]
    w3buf = [QP[:, 0:4096], QP[:, 4096:8192]]
    x2f_use = None
    w1_use = [None, None]; w3_use = [None, None]
    w2_use = [None, None]
    x2tm_use = [None, None]
    sil_use = [None, None]
    gprev = None
    sums_use = [None] * 4
    pbu = {i: None for i in range(8)}
    sq_use = None
    out_use = None
    last_out = None
    ev["x2_f"] = mirror_pass(mk, D_, "x2_s", "x2_f", 16,
                             [CT[:, 0:2048], CT[:, 2048:4096]], ev["x2_s"])
    pref = {}
    for grp in range(4):
        if grp in pref:
            x2_ld, w1_ld0, w3_ld0 = pref.pop(grp)
        else:
            mk.w(S, ev["x2_f"], x2f_use)
            x2_ld = S.emit(lambda e, grp=grp: e.dma_start(
                out=x2f.rearrange("p (dc n) -> p dc n", n=512),
                in_=D_["x2_f"].ap().rearrange("(dc p) t -> p dc t", p=128)[
                    :, :, grp * 512:(grp + 1) * 512]), mk.dq("x2fl"))
            w1_ld0 = w3_ld0 = None
        e_gt_last = None
        for pr in range(32):
            wp = pr % 2
            if pr == 0 and w1_ld0 is not None:
                w1_ld, w3_ld = w1_ld0, w3_ld0
            else:
                mk.w(S, ev["w1t_f"], w1_use[wp])
                w1_ld = S.emit(lambda e, pr=pr, wp=wp: e.dma_start(
                    out=w1buf[wp].rearrange("p (dc n) -> p dc n", n=256),
                    in_=D_["w1t_f"].ap().rearrange("(dc p) t -> p dc t", p=128)[
                        :, :, pr * 256:(pr + 1) * 256]), mk.dq(f"w1l{wp}"))
                mk.w(S, ev["w3t_f"], w3_use[wp])
                w3_ld = S.emit(lambda e, pr=pr, wp=wp: e.dma_start(
                    out=w3buf[wp].rearrange("p (dc n) -> p dc n", n=256),
                    in_=D_["w3t_f"].ap().rearrange("(dc p) t -> p dc t", p=128)[
                        :, :, pr * 256:(pr + 1) * 256]), mk.dq(f"w3l{wp}"))
            for hl in range(2):
                hs = pr * 2 + hl
                bk = hs % 2
                mk.w(T, x2_ld, w1_ld, pbu[bk], gprev)
                for dc in range(16):
                    mmUp = T.emit(lambda e, dc=dc, hl=hl, bk=bk, wp=wp: e.matmul(
                        pb[bk][:, :], w1buf[wp][:, dc * 256 + hl * 128:dc * 256 + (hl + 1) * 128],
                        x2f[:, dc * 512:(dc + 1) * 512],
                        start=(dc == 0), stop=(dc == 15)), cT if dc == 15 else None)
                w1_use[wp] = mmUp
                mk.w(T, w3_ld, pbu[2 + bk])
                for dc in range(16):
                    mmL = T.emit(lambda e, dc=dc, hl=hl, bk=bk, wp=wp: e.matmul(
                        pb[2 + bk][:, :], w3buf[wp][:, dc * 256 + hl * 128:dc * 256 + (hl + 1) * 128],
                        x2f[:, dc * 512:(dc + 1) * 512],
                        start=(dc == 0), stop=(dc == 15)), cT if dc == 15 else None)
                w3_use[wp] = mmL
                mk.w(S, mmUp, sil_use[bk])
                e_si = S.emit(lambda e, bk=bk: e.activation(
                    sil[bk], pb[bk][:, :], AF.Silu), cS)
                pbu[bk] = e_si
                mk.w(V, e_si, mmL, gprev)
                e_gt = V.emit(lambda e, hs=hs, bk=bk: e.tensor_tensor(
                    XF[:, hs * 512:(hs + 1) * 512], sil[bk], pb[2 + bk][:, :],
                    ALU.mult), cV)
                pbu[2 + bk] = e_gt
                sil_use[bk] = e_gt
                e_gt_last = e_gt
        x2f_use = mmL
        if grp < 3:
            mk.w(Dq, x2f_use)
            nx_x2 = Dq.emit(lambda e, grp=grp: e.dma_start(
                out=x2f.rearrange("p (dc n) -> p dc n", n=512),
                in_=D_["x2_f"].ap().rearrange("(dc p) t -> p dc t", p=128)[
                    :, :, (grp + 1) * 512:(grp + 2) * 512]), mk.dq("x2fl"))
            mk.w(Dq, w1_use[0])
            nx_w1 = Dq.emit(lambda e: e.dma_start(
                out=w1buf[0].rearrange("p (dc n) -> p dc n", n=256),
                in_=D_["w1t_f"].ap().rearrange("(dc p) t -> p dc t", p=128)[
                    :, :, 0:256]), mk.dq("w1l0"))
            mk.w(Dq, w3_use[0])
            nx_w3 = Dq.emit(lambda e: e.dma_start(
                out=w3buf[0].rearrange("p (dc n) -> p dc n", n=256),
                in_=D_["w3t_f"].ap().rearrange("(dc p) t -> p dc t", p=128)[
                    :, :, 0:256]), mk.dq("w3l0"))
            pref[grp + 1] = (nx_x2, nx_w1, nx_w3)
        mmD_ = None
        for ds in range(4):
            for hg in range(32):
                wp = hg % 2
                mk.w(S, ev_w2f, w2_use[wp])
                w2_ld = S.emit(lambda e, hg=hg, ds=ds, wp=wp: e.dma_start(
                    out=w2tile[wp].rearrange("p (c n) -> p c n", n=512),
                    in_=D_["w2f"].ap().rearrange("(c p) n -> p c n", p=128)[
                        :, hg * 2:(hg + 1) * 2, ds * 512:(ds + 1) * 512]),
                    mk.dq(f"w2t{wp}"))
                for hs in range(2):
                    hc = hg * 2 + hs
                    for tcl in range(4):
                        bk = 7 - tcl
                        mk.w(T, w2_ld, e_gt_last)
                        if hc == 0:
                            mk.w(T, pbu[bk])
                        mmD_ = T.emit(lambda e, hc=hc, tcl=tcl, bk=bk, wp=wp, hs=hs: e.matmul(
                            pb[bk][:, :], XF[:, hc * 512 + tcl * 128:hc * 512 + (tcl + 1) * 128],
                            w2tile[wp][:, hs * 512:(hs + 1) * 512],
                            start=(hc == 0), stop=(hc == 63)),
                            cT if (hc == 63 or tcl == 3) else None)
                w2_use[wp] = mmD_
            for tcl in range(4):
                bk = 7 - tcl
                tok0 = grp * 512 + tcl * 128
                mk.w(S, ev["x2_s"], x2tm_use[tcl % 2])
                x2t_ld = S.emit(lambda e, tok0=tok0, ds=ds, tcl=tcl: e.dma_start(
                    out=x2tm[tcl % 2], in_=D_["x2_s"].ap()[tok0:tok0 + 128,
                                                           ds * 512:(ds + 1) * 512]),
                    mk.dq(f"x2tm{tcl % 2}"))
                mk.w(V, mmD_, x2t_ld, sums_use[tcl])
                e_ep = V.emit(lambda e, tcl=tcl, ds=ds, bk=bk: e.tensor_add(
                    sums[tcl][:, ds * 512:(ds + 1) * 512], pb[bk][:, :],
                    x2tm[tcl % 2]), cV)
                pbu[bk] = e_ep
                x2tm_use[tcl % 2] = e_ep
                if DBG_P4 == grp and tcl == 2:
                    mk.w(V, e_ep)
                    V.emit(lambda e, ds=ds: e.tensor_copy(
                        D_["dbgf"][:, ds * 256:(ds + 1) * 256], pb[6][:, 0:256]), None)
                    e_cap = V.emit(lambda e, ds=ds, tcl=tcl: e.tensor_copy(
                        D_["dbgb"][:, ds * 256:(ds + 1) * 256], x2tm[tcl % 2][:, 0:256]), cV)
                    pbu[bk] = e_cap
                    x2tm_use[tcl % 2] = e_cap
        gprev = mmD_
        for tcl in range(4):
            bk2 = 7 - tcl
            tok0 = grp * 512 + tcl * 128
            mk.w(S, pbu[bk2], sq_use)
            e_sq = S.emit(lambda e, tcl=tcl: e.activation(
                sqb, sums[tcl], AF.Square), cS)
            mk.w(V, pbu[bk2])
            e_r1 = V.emit(lambda e, tcl=tcl: e.reduce_sum(
                D_["smallf"][:, 0:1], sums[tcl].rearrange("p (a d) -> p a d", a=1),
                axis=AX.X), cV)
            mk.w(V, e_sq)
            e_r2 = V.emit(lambda e: e.reduce_sum(
                D_["smallf"][:, 1:2], sqb.rearrange("p (a d) -> p a d", a=1),
                axis=AX.X), cV)
            sq_use = e_r2
            e_rstd, e_m = ln_newton(mk, D_, e_r1, e_r2, 1.0 / D)
            mk.w(V, e_rstd, out_use)
            e_ap = V.emit(lambda e, tcl=tcl: e.tensor_scalar(
                outst, sums[tcl], D_["smallf"][:, 32:33], D_["smallf"][:, 112:113],
                ALU.subtract, ALU.mult), cV)
            sums_use[tcl] = e_ap
            pbu[bk2] = e_ap
            mk.w(S, e_ap)
            last_out = S.emit(lambda e, tok0=tok0: e.dma_start(
                out=D_["out"][tok0:tok0 + 128, :], in_=outst), mk.dq("outst"))
            out_use = last_out
    mk.w(Dq, last_out)
    mk.w(S, last_out)
    return ev


def emit_debug_taps(mk, D_):
    if not DEBUG_TAPS:
        return
    F = fence_all(mk)
    prev = None
    for name in DEBUG_TAPS:
        t = D_[name]
        ap = t.ap() if hasattr(t, "ap") else t[:, :]
        shp = [int(s) for s in ap.shape]
        dt = ap.dtype
        o = mk.dout("dbg_" + name, shp, dt)
        mk.w(mk.P, prev)
        prev = mk.P.emit(lambda e, o=o, ap=ap: e.dma_start(out=o, in_=ap), mk.dq("dbg"))
    mk.w(mk.P, prev)


_ONES = np.ones((128, 128), dtype=np.float32)
_j = np.arange(128)
_UL = (_j[:, None] <= _j[None, :]).astype(np.float32)


def _consts():
    h128 = (hadamard(128) / np.sqrt(2.0)).astype(ml_dtypes.bfloat16)
    h64 = (hadamard(64) / 64.0).astype(ml_dtypes.bfloat16)
    tri = _UL.astype(ml_dtypes.bfloat16)
    return h128, h64, tri


def kernel(**inputs):
    pre = "L0_attn_"
    x = np.asarray(inputs[pre + "hidden_states"], dtype=np.float32)
    h128, h64, tri = _consts()
    base = dict(
        wq=np.ascontiguousarray(inputs[pre + "Wq"], dtype=np.float32),
        wk=np.ascontiguousarray(inputs[pre + "Wk"], dtype=np.float32),
        wv=np.ascontiguousarray(inputs[pre + "Wv"], dtype=np.float32),
        wg=np.ascontiguousarray(inputs[pre + "Wg"], dtype=np.float32),
        wo=np.ascontiguousarray(inputs[pre + "Wo"], dtype=np.float32),
        wb=np.ascontiguousarray(inputs[pre + "Wb"], dtype=np.float32),
        bb=np.ascontiguousarray(inputs[pre + "bb"], dtype=np.float32),
        qcw=np.ascontiguousarray(inputs[pre + "qconv_w"], dtype=np.float32),
        qcb=np.ascontiguousarray(inputs[pre + "qconv_b"], dtype=np.float32),
        kcw=np.ascontiguousarray(inputs[pre + "kconv_w"], dtype=np.float32),
        kcb=np.ascontiguousarray(inputs[pre + "kconv_b"], dtype=np.float32),
        qnw=np.ascontiguousarray(inputs[pre + "qn_w"], dtype=np.float32),
        qnb=np.ascontiguousarray(inputs[pre + "qn_b"], dtype=np.float32),
        knw=np.ascontiguousarray(inputs[pre + "kn_w"], dtype=np.float32),
        knb=np.ascontiguousarray(inputs[pre + "kn_b"], dtype=np.float32),
        w1=np.ascontiguousarray(inputs["L0_ffn_W1"], dtype=np.float32),
        w3=np.ascontiguousarray(inputs["L0_ffn_W3"], dtype=np.float32),
        w2=np.ascontiguousarray(inputs["L0_ffn_W2"], dtype=np.float32),
        c_ones=_ONES, c_ul=_UL, c_tri=tri, c_h128=h128, c_h64=h64,
        c_eye=np.eye(128, dtype=np.float32),
    )
    for k in ("bb",):
        base[k] = base[k].reshape(1, H)
    for k in ("qnw", "qnb", "knw", "knb"):
        base[k] = base[k].reshape(1, DKH)
    in_maps = []
    for c in range(NCORE):
        b, s0 = c // 4, (c % 4) * TPC
        xs = np.zeros((TPAD, D), dtype=np.float32)
        lo = max(0, s0 - HALO)
        xs[HALO - (s0 - lo):] = x[b, lo:s0 + TPC]
        pm = np.zeros(8, dtype=np.float32)
        for c2 in range(8):
            if c2 // 4 == b and c2 < c:
                pm[c2] = 1.0
        m = dict(base)
        m["x"] = xs
        m["c_pm"] = pm.reshape(1, 8)
        in_maps.append(m)

    nc = build_program()
    nc.has_collectives = True
    res = run_bass_kernel_spmd(nc, in_maps, core_ids=list(range(NCORE)))
    global _LAST_RESULTS
    _LAST_RESULTS = res.results
    y = np.concatenate([res.results[c]["out"] for c in range(NCORE)], axis=0)
    return y.reshape(B, L, D)


_LAST_RESULTS = None


if __name__ == "__main__":
    nc = build_program()
    print("build ok")
